# version 15
# baseline (speedup 1.0000x reference)
import numpy as np
from contextlib import ExitStack

import concourse.bass as bass
import concourse.mybir as mybir
from concourse.bass_utils import run_bass_kernel_spmd

F32 = mybir.dt.float32
BF16 = mybir.dt.bfloat16
AF = mybir.ActivationFunctionType
ALU = mybir.AluOpType
AX = mybir.AxisListType

T = 4096
D = 2048
NH = 16
DH = 128
FF = 8192
NB = 16
BS = 512
TILE = 512
NTILE = T // TILE
DC = D // 128
FC = FF // 128
ALPHA = float(128.0 ** -0.25)
EPS = 1e-5


class Sem:

    def __init__(self, handle):
        self.h = handle
        self.n = 0

    def tgt(self, k):
        self.n += k
        return self.n


class Prog:

    KEYS = ("te", "act", "dve", "pool", "sp")

    def __init__(self):
        self.q = {k: [] for k in self.KEYS}
        self.hw = {}

    def te(self, f):
        self.q["te"].append(f)

    def act(self, f):
        self.q["act"].append(f)

    def dve(self, f):
        self.q["dve"].append(f)

    def pool(self, f):
        self.q["pool"].append(f)

    def sp(self, f):
        self.q["sp"].append(f)

    def wait(self, key, sem, n):
        if n <= 0 or self.hw.get((key, sem.h.num), -1) >= n:
            return
        self.hw[(key, sem.h.num)] = n
        self.q[key].append(lambda e, _h=sem.h, _n=n: e.wait_ge(_h, _n))


def build(debug=False):
    nc = bass.Bass("TRN2", target_bir_lowering=False)

    x_fm = nc.dram_tensor("x_fm", [D, T], F32, kind="ExternalInput").ap()
    wq = nc.dram_tensor("wq", [D, D], F32, kind="ExternalInput").ap()
    wk = nc.dram_tensor("wk", [D, D], F32, kind="ExternalInput").ap()
    wv = nc.dram_tensor("wv", [D, D], F32, kind="ExternalInput").ap()
    wo = nc.dram_tensor("wo", [D, D], F32, kind="ExternalInput").ap()
    w1t = nc.dram_tensor("w1t", [D, FF], F32, kind="ExternalInput").ap()
    w2p = nc.dram_tensor("w2p", [DC, 128, FC * 128], F32, kind="ExternalInput").ap()
    wrt = nc.dram_tensor("wrt", [D, NB], F32, kind="ExternalInput").ap()
    bq2 = nc.dram_tensor("bq2", [128, 16], F32, kind="ExternalInput").ap()
    bk2 = nc.dram_tensor("bk2", [128, 16], F32, kind="ExternalInput").ap()
    bv2 = nc.dram_tensor("bv2", [128, 16], F32, kind="ExternalInput").ap()
    bo2 = nc.dram_tensor("bo2", [128, 16], F32, kind="ExternalInput").ap()
    brr = nc.dram_tensor("brr", [1, NB], F32, kind="ExternalInput").ap()
    causal = nc.dram_tensor("causal", [128, 4 * TILE], F32, kind="ExternalInput").ap()
    identf = nc.dram_tensor("identf", [128, 128], F32, kind="ExternalInput").ap()
    self_ = nc.dram_tensor("self_", [NB, NB * 128], F32, kind="ExternalInput").ap()

    y = nc.dram_tensor("y", [D, T], F32, kind="ExternalOutput").ap()

    s_w1 = nc.dram_tensor("s_w1", [D, FF], BF16, kind="Internal").ap()
    s_w2 = nc.dram_tensor("s_w2", [DC, 128, FC * 128], BF16, kind="Internal").ap()
    s_xn = nc.dram_tensor("s_xn", [D, T], BF16, kind="Internal").ap()
    s_q = nc.dram_tensor("s_q", [D, T], BF16, kind="Internal").ap()
    s_k = nc.dram_tensor("s_k", [D, T], BF16, kind="Internal").ap()
    s_v = nc.dram_tensor("s_v", [NH, 128, T // 128, DH], BF16, kind="Internal").ap()
    s_attn = nc.dram_tensor("s_attn", [D, T], BF16, kind="Internal").ap()
    s_ao = nc.dram_tensor("s_ao", [D, T], F32, kind="Internal").ap()

    p = Prog()

    with ExitStack() as gctx:
        sbuf = lambda n, s, d: gctx.enter_context(nc.sbuf_tensor(n, s, d))
        mksem = lambda n: Sem(gctx.enter_context(nc.semaphore(n)))

        idb = sbuf("idb", [128, 128], BF16)
        selb = sbuf("selb", [NB, NB * 128], BF16)
        wrtb = sbuf("wrtb", [128, DC * NB], BF16)
        brr_bf = sbuf("brr_bf", [1, NB], BF16)
        ones1_bf = sbuf("ones1_bf", [1, 128], BF16)
        ones128_bf = sbuf("ones128_bf", [128, 1], BF16)
        ones1_f = sbuf("ones1_f", [1, 128], F32)
        epsc = sbuf("epsc", [128, 1], F32)
        bqs = sbuf("bqs", [128, 16], F32)
        bks = sbuf("bks", [128, 16], F32)
        bvs = sbuf("bvs", [128, 16], F32)
        bos = sbuf("bos", [128, 16], F32)
        bqsc = sbuf("bqsc", [128, 16], F32)
        bksc = sbuf("bksc", [128, 16], F32)
        mask_fm = sbuf("mask_fm", [NB, T], BF16)
        qknorm = sbuf("qknorm", [1, 2], F32)
        nrm2 = sbuf("nrm2", [1, 2], F32)
        nrm_row = sbuf("nrm_row", [1, TILE], F32)
        negrow = sbuf("negrow", [1, 1], F32)
        negb = sbuf("negb", [128, 1], F32)
        q2c = [sbuf("q2c0", [128, TILE], BF16), sbuf("q2c1", [128, TILE], BF16)]

        g_ld = mksem("g_ld")
        g_dve = mksem("g_dve")
        phase = mksem("phase")

        def dma(queue, out, in_, sem, k=16):
            t = sem.tgt(k)
            p.q[queue].append(lambda e, _o=out, _i=in_, _h=sem.h, _k=k:
                              e.dma_start(out=_o, in_=_i).then_inc(_h, _k))
            return t

        dma("pool", idb[:], identf[:, :], g_ld)
        t_idb = g_ld.n

        dma("sp", bqs[:], bq2[:, :], g_ld)
        dma("sp", bks[:], bk2[:, :], g_ld)
        dma("sp", bvs[:], bv2[:, :], g_ld)
        dma("sp", bos[:], bo2[:, :], g_ld)
        t_pro = g_ld.n

        def dve_op(f, sem=None):
            if sem is None:
                p.dve(f)
                return None
            t = sem.tgt(1)
            p.dve(lambda e, _f=f, _h=sem.h: _f(e).then_inc(_h, 1))
            return t

        p.wait("dve", g_ld, t_pro)
        dve_op(lambda e: e.memset(qknorm[:], 0.0))
        dve_op(lambda e: e.memset(ones1_bf[:], 1.0))
        dve_op(lambda e: e.memset(ones128_bf[:], 1.0))
        dve_op(lambda e: e.memset(ones1_f[:], 1.0))
        dve_op(lambda e: e.memset(epsc[:], EPS))
        dve_op(lambda e: e.tensor_scalar_mul(bqsc[:], bqs[:], ALPHA))
        t_bias = dve_op(lambda e: e.tensor_scalar_mul(bksc[:], bks[:], ALPHA), g_dve)

        s_w1_ld = mksem("s_w1_ld")
        s_w2_ld = mksem("s_w2_ld")

        w_ld = mksem("w_ld")
        xb_ld = [mksem("xb_ld0"), mksem("xb_ld1")]
        xn_sp = mksem("xn_sp")
        qk_st = mksem("qk_st")
        a_pe = mksem("a_pe")
        a_act = mksem("a_act")
        a_dve = mksem("a_dve")
        xn_rld = [mksem("xn_rld0"), mksem("xn_rld1")]
        k_ld = [mksem("k_ld0"), mksem("k_ld1")]
        v_ld = [mksem("v_ld0"), mksem("v_ld1")]

        with ExitStack() as actx:
            asb = lambda n, s, d: actx.enter_context(nc.sbuf_tensor(n, s, d))
            aps = lambda n, s, d: actx.enter_context(nc.psum_tensor(n, s, d))
            wslotA = asb("wslotA", [128, DC * D], BF16)
            wslotB = asb("wslotB", [128, DC * D], BF16)
            xnf = [asb("xnf0", [128, DC * TILE], BF16), asb("xnf1", [128, DC * TILE], BF16)]
            xsq = asb("xsq", [128, 8 * TILE], BF16)
            t1 = asb("t1", [128, 2 * TILE], BF16)
            mrs = asb("mrs", [128, 2 * TILE], BF16)
            rows = asb("rows", [1, 3 * TILE], F32)
            rows_bf = asb("rows_bf", [1, 2 * TILE], BF16)
            r_sb = asb("r_sb", [128, NB], F32)
            max8 = asb("max8", [128, 8], F32)
            mask_tm = asb("mask_tm", [128, NB], BF16)
            qkv_st = [asb("qkv_st0", [128, TILE], BF16), asb("qkv_st1", [128, TILE], BF16)]
            vt_st = [asb("vt_st0", [128, TILE], BF16), asb("vt_st1", [128, TILE], BF16)]
            jnkd = asb("jnkd", [128, 1], F32)
            jnka = asb("jnka", [128, 1], F32)

            ps_qkv = [aps("ps_qkv0", [128, TILE], F32), aps("ps_qkv1", [128, TILE], F32)]
            ps_r = aps("ps_r", [128, NB], F32)
            ps_mT = aps("ps_mT", [128, TILE], BF16)
            ps_bc = aps("ps_bc", [128, TILE], F32)
            ps_sum = aps("ps_sum", [1, TILE], F32)
            ps_sq = aps("ps_sq", [1, TILE], F32)
            ps_nrm = aps("ps_nrm", [1, TILE], F32)

            def dve_mile():
                t = a_dve.tgt(1)
                p.dve(lambda e: e.tensor_copy(out=jnkd[:], in_=epsc[:]).then_inc(a_dve.h, 1))
                return t

            def dve_fence():
                t = dve_mile()
                p.wait("dve", a_dve, t)

            def act_mile():
                t = a_act.tgt(1)
                p.act(lambda e: e.copy(out=jnka[:], in_=epsc[:]).then_inc(a_act.h, 1))
                return t

            dma("pool", selb[:], self_[:, :], g_ld)
            dma("pool", brr_bf[:], brr[:, :], g_ld)
            for dc in range(DC):
                dma("pool", wrtb[:, dc * NB:(dc + 1) * NB],
                    wrt[dc * 128:(dc + 1) * 128, :], g_ld)
            t_pro_pool = g_ld.n

            xb_half = {}
            sq_mile = {}
            sq_pass_pe = {}
            t_statdone = {}
            t_rows_free = {-1: 0}
            t_rstd_dve = {}
            t_norm_done = {0: 0}
            t_mrs_ok = {}
            st_bc_free = [0]
            st_bc_pe = [0]
            st_r_evac = [0]
            st_mask_dve = [0]
            st_maskT_pe = [0]
            st_maskT_evac = [0]

            xnf_free_t = [0, 0]
            xn_spill_t = [0, 0]

            def emit_xb_load(t):
                nx = t % 2
                p.wait("pool", a_pe, xnf_free_t[nx])
                p.wait("pool", xn_sp, xn_spill_t[nx])
                hts = []
                for half in range(2):
                    tt = dma("pool",
                             xnf[nx][:].rearrange("p (c t) -> p c t", c=DC)[:, half * 8:(half + 1) * 8, :],
                             x_fm.rearrange("(c p) t -> p c t", p=128)[:, half * 8:(half + 1) * 8,
                                                                      t * TILE:(t + 1) * TILE],
                             xb_ld[nx])
                    hts.append(tt)
                xb_half[t] = hts

            def emit_squares(t, grp):
                nx = t % 2
                half = 0 if grp < 2 else 1
                p.wait("act", xb_ld[nx], xb_half[t][half])
                prev = (t, grp - 2) if grp >= 2 else (t - 1, grp + 2)
                p.wait("act", a_pe, sq_pass_pe.get(prev, 0))
                for c4 in range(4):
                    c = grp * 4 + c4
                    slot = (grp % 2) * 4 + c4
                    p.act(lambda e, _nx=nx, _c=c, _s=slot: e.activation(
                        out=xsq[:, _s * TILE:(_s + 1) * TILE],
                        in_=xnf[_nx][:, _c * TILE:(_c + 1) * TILE], func=AF.Square))
                sq_mile[(t, grp)] = act_mile()

            def emit_sum_pass(t, c):
                nx = t % 2
                p.wait("te", xb_ld[nx], xb_half[t][0 if c < 8 else 1])
                if c == 0:
                    p.wait("te", a_dve, t_rows_free[t - 1])
                if c == DC - 1:
                    a_pe.tgt(1)
                    p.te(lambda e, _nx=nx, _c=c: e.matmul(
                        ps_sum[0:1, :], ones128_bf[:, 0:1],
                        xnf[_nx][:, _c * TILE:(_c + 1) * TILE],
                        start=(_c == 0), stop=True).then_inc(a_pe.h, 1))
                else:
                    p.te(lambda e, _nx=nx, _c=c: e.matmul(
                        ps_sum[0:1, :], ones128_bf[:, 0:1],
                        xnf[_nx][:, _c * TILE:(_c + 1) * TILE],
                        start=(_c == 0), stop=False))

            def emit_sq_pass(t, c):
                grp = c // 4
                slot = (grp % 2) * 4 + c % 4
                p.wait("te", a_act, sq_mile[(t, grp)])
                if c == 0:
                    p.wait("te", a_dve, t_rows_free[t - 1])
                if c % 4 == 3:
                    tp = a_pe.tgt(1)
                    p.te(lambda e, _s=slot, _c=c: e.matmul(
                        ps_sq[0:1, :], ones128_bf[:, 0:1],
                        xsq[:, _s * TILE:(_s + 1) * TILE],
                        start=(_c == 0), stop=(_c == DC - 1)).then_inc(a_pe.h, 1))
                    sq_pass_pe[(t, grp)] = tp
                    if c == DC - 1:
                        t_statdone[t] = tp
                else:
                    p.te(lambda e, _s=slot, _c=c: e.matmul(
                        ps_sq[0:1, :], ones128_bf[:, 0:1],
                        xsq[:, _s * TILE:(_s + 1) * TILE],
                        start=(_c == 0), stop=False))

            def emit_rows(t):
                p.wait("dve", a_pe, t_statdone[t])
                p.wait("dve", a_pe, st_bc_pe[0])
                dve_op(lambda e: e.tensor_scalar_mul(rows_bf[0:1, 0:TILE], ps_sum[0:1, :], 1.0 / D))
                dve_op(lambda e: e.tensor_scalar_mul(rows[0:1, 0:TILE], ps_sum[0:1, :], 1.0 / D))
                dve_fence()
                dve_op(lambda e: e.tensor_mul(rows[0:1, TILE:2 * TILE],
                                              rows[0:1, 0:TILE], rows[0:1, 0:TILE]))
                dve_fence()
                dve_op(lambda e: e.scalar_tensor_tensor(
                    out=rows[0:1, 0:TILE], in0=ps_sq[0:1, :], scalar=1.0 / D,
                    in1=rows[0:1, TILE:2 * TILE], op0=ALU.mult, op1=ALU.subtract))
                t_rows_free[t] = dve_mile()
                p.wait("act", a_dve, t_rows_free[t])
                p.act(lambda e: e.activation(out=rows[0:1, TILE:2 * TILE],
                                             in_=rows[0:1, 0:TILE],
                                             func=AF.Sqrt, bias=epsc[0:1, 0:1]))
                ta = act_mile()
                p.wait("dve", a_act, ta)
                dve_op(lambda e: e.reciprocal(rows[0:1, 2 * TILE:3 * TILE],
                                              rows[0:1, TILE:2 * TILE]))
                dve_fence()
                dve_op(lambda e: e.tensor_copy(out=rows_bf[0:1, TILE:2 * TILE],
                                               in_=rows[0:1, 2 * TILE:3 * TILE]))
                t_rstd_dve[t] = dve_mile()

            def emit_bcast(t):
                p.wait("te", a_dve, t_rows_free[t])
                p.wait("te", a_act, st_bc_free[0])
                tp = a_pe.tgt(1)
                p.te(lambda e: e.matmul(ps_bc[:, :], ones1_bf[0:1, :], rows_bf[0:1, 0:TILE],
                                        start=True, stop=True).then_inc(a_pe.h, 1))
                p.wait("act", a_pe, tp)
                p.wait("act", a_dve, t_norm_done.get(t - 1, 0))
                p.act(lambda e: e.copy(out=mrs[:, 0:TILE], in_=ps_bc[:, :]))
                tm = act_mile()
                p.wait("te", a_act, tm)
                p.wait("te", a_dve, t_rstd_dve[t])
                tp2 = a_pe.tgt(1)
                p.te(lambda e: e.matmul(ps_bc[:, :], ones1_bf[0:1, :],
                                        rows_bf[0:1, TILE:2 * TILE],
                                        start=True, stop=True).then_inc(a_pe.h, 1))
                st_bc_pe[0] = tp2
                p.wait("act", a_pe, tp2)
                p.act(lambda e: e.copy(out=mrs[:, TILE:2 * TILE], in_=ps_bc[:, :]))
                st_bc_free[0] = act_mile()
                t_mrs_ok[t] = st_bc_free[0]

            def emit_norm_quarter(t, qtr):
                nx = t % 2
                p.wait("dve", a_act, t_mrs_ok[t])
                p.wait("dve", a_pe, t_statdone[t])
                p.wait("dve", a_act, sq_mile[(t, 3)])
                for pr in range(2):
                    for c2 in range(2):
                        c = qtr * 4 + pr * 2 + c2
                        p.dve(lambda e, _nx=nx, _c=c, _c2=c2: e.tensor_tensor(
                            out=t1[:, _c2 * TILE:(_c2 + 1) * TILE],
                            in0=xnf[_nx][:, _c * TILE:(_c + 1) * TILE],
                            in1=mrs[:, 0:TILE], op=ALU.subtract))
                    dve_fence()
                    for c2 in range(2):
                        c = qtr * 4 + pr * 2 + c2
                        p.dve(lambda e, _nx=nx, _c=c, _c2=c2: e.tensor_tensor(
                            out=xnf[_nx][:, _c * TILE:(_c + 1) * TILE],
                            in0=t1[:, _c2 * TILE:(_c2 + 1) * TILE],
                            in1=mrs[:, TILE:2 * TILE], op=ALU.mult))
                if qtr == 3:
                    t_norm_done[t] = dve_mile()

            xn_spill_tile = {}

            def emit_spill(t):
                nx = t % 2
                p.wait("sp", a_dve, t_norm_done[t])
                for half in range(2):
                    xn_spill_t[nx] = dma(
                        "sp",
                        s_xn.rearrange("(c p) t -> p c t", p=128)[:, half * 8:(half + 1) * 8,
                                                                 t * TILE:(t + 1) * TILE],
                        xnf[nx][:].rearrange("p (c t) -> p c t", c=DC)[:, half * 8:(half + 1) * 8, :],
                        xn_sp)
                xn_spill_tile[t] = xn_spill_t[nx]

            def emit_router_sub(t, s):
                nx = t % 2
                sub = t * 4 + s
                p.wait("te", a_dve, t_norm_done[t])
                p.wait("te", g_ld, t_pro_pool)
                p.wait("te", a_act, st_r_evac[0])
                for dc in range(DC):
                    p.te(lambda e, _s=s, _dc=dc, _nx=nx: e.matmul(
                        ps_r[:], xnf[_nx][:, _dc * TILE + _s * 128: _dc * TILE + (_s + 1) * 128],
                        wrtb[:, _dc * NB:(_dc + 1) * NB], start=(_dc == 0), stop=False))
                tp_r = a_pe.tgt(1)
                p.te(lambda e: e.matmul(ps_r[:], ones1_bf[0:1, :], brr_bf[0:1, :],
                                        start=False, stop=True).then_inc(a_pe.h, 1))
                p.wait("act", a_pe, tp_r)
                p.wait("act", a_dve, st_mask_dve[0])
                p.act(lambda e: e.copy(out=r_sb[:], in_=ps_r[:]))
                ta_r = act_mile()
                st_r_evac[0] = ta_r
                p.wait("dve", a_act, ta_r)
                p.wait("dve", a_pe, st_maskT_pe[0])
                dve_op(lambda e: e.max(out=max8[:], in_=r_sb[:]))
                dve_fence()
                p.dve(lambda e: e.tensor_scalar(
                    out=mask_tm[:], in0=r_sb[:], scalar1=max8[:, 3:4], scalar2=None,
                    op0=ALU.is_ge))
                td_m = dve_mile()
                st_mask_dve[0] = td_m
                p.wait("te", a_dve, td_m)
                p.wait("te", a_act, st_maskT_evac[0])
                tp_m = a_pe.tgt(1)
                p.te(lambda e: e.transpose(ps_mT[0:NB, 0:128], mask_tm[:],
                                           idb[:]).then_inc(a_pe.h, 1))
                st_maskT_pe[0] = tp_m
                p.wait("act", a_pe, tp_m)
                p.act(lambda e, _sub=sub: e.copy(
                    out=mask_fm[:, _sub * 128:(_sub + 1) * 128], in_=ps_mT[0:NB, 0:128]))
                st_maskT_evac[0] = act_mile()

            qkv_free_t = [0, 0]
            vt_free_t = [0, 0]
            ps_qkv_free = [0, 0]
            ps_vt_free = [0]
            k0_st = [0]
            v0_st = [0]
            nrm_sq_free = [0, 0]
            nrm_row_evac = 0
            pending_norm = [None, None]

            def flush_norm(par2, wi):
                nonlocal nrm_row_evac
                if pending_norm[par2] is None:
                    return
                td_sq = pending_norm[par2]
                pending_norm[par2] = None
                p.wait("te", a_dve, td_sq)
                p.wait("te", a_act, nrm_row_evac)
                tp_n = a_pe.tgt(1)
                p.te(lambda e, _p2=par2: e.matmul(
                    ps_nrm[0:1, :], ones128_bf[:, 0:1], q2c[_p2][:],
                    start=True, stop=True).then_inc(a_pe.h, 1))
                nrm_sq_free[par2] = tp_n
                p.wait("act", a_pe, tp_n)
                p.act(lambda e: e.copy(out=nrm_row[0:1, :], in_=ps_nrm[0:1, :]))
                nrm_row_evac = act_mile()
                p.wait("dve", a_act, nrm_row_evac)
                dve_op(lambda e: e.reduce_max(nrm2[0:1, 0:1], nrm_row[0:1, :], axis=AX.X))
                dve_fence()
                dve_op(lambda e, _wi=wi: e.tensor_tensor(
                    out=qknorm[0:1, _wi:_wi + 1], in0=qknorm[0:1, _wi:_wi + 1],
                    in1=nrm2[0:1, 0:1], op=ALU.max))
                dve_fence()

            def emit_qkv(j, px, kind, wsl, hook=None):
                dst = {"q": s_q, "k": s_k, "v": None}[kind]
                bias = {"q": bqsc, "k": bksc, "v": bvs}[kind]
                scale = ALPHA if kind in ("q", "k") else 1.0
                wi = 0 if kind == "q" else 1
                for oc in range(DC):
                    par2 = oc % 2
                    if hook is not None:
                        hook(oc)
                    if kind in ("q", "k"):
                        flush_norm(par2, wi)
                    p.wait("te", a_act, ps_qkv_free[par2])
                    for dc in range(DC):
                        last = dc == DC - 1
                        if last:
                            a_pe.tgt(1)
                            p.te(lambda e, _oc=oc, _dc=dc, _px=px, _p2=par2, _w=wsl: e.matmul(
                                ps_qkv[_p2][:],
                                _w[:, _dc * D + _oc * 128: _dc * D + (_oc + 1) * 128],
                                xnf[_px][:, _dc * TILE:(_dc + 1) * TILE],
                                start=(_dc == 0), stop=True).then_inc(a_pe.h, 1))
                        else:
                            p.te(lambda e, _oc=oc, _dc=dc, _px=px, _p2=par2, _w=wsl: e.matmul(
                                ps_qkv[_p2][:],
                                _w[:, _dc * D + _oc * 128: _dc * D + (_oc + 1) * 128],
                                xnf[_px][:, _dc * TILE:(_dc + 1) * TILE],
                                start=(_dc == 0), stop=False))
                    t_mm = a_pe.n
                    p.wait("act", a_pe, t_mm)
                    p.wait("act", qk_st, qkv_free_t[par2])
                    p.act(lambda e, _p2=par2, _oc=oc, _b=bias, _sc=scale: e.activation(
                        out=qkv_st[_p2][:], in_=ps_qkv[_p2][:], func=AF.Identity,
                        bias=_b[:, _oc:_oc + 1], scale=_sc))
                    ta_e = act_mile()
                    ps_qkv_free[par2] = ta_e
                    if kind in ("q", "k"):
                        p.wait("sp", a_act, ta_e)
                        qkv_free_t[par2] = dma(
                            "sp", dst[oc * 128:(oc + 1) * 128, j * TILE:(j + 1) * TILE],
                            qkv_st[par2][:], qk_st)
                        if kind == "k" and oc == 0:
                            k0_st[0] = qkv_free_t[par2]
                        p.wait("dve", a_act, ta_e)
                        p.wait("dve", a_pe, nrm_sq_free[par2])
                        p.dve(lambda e, _p2=par2: e.tensor_tensor(
                            out=q2c[_p2][:], in0=qkv_st[_p2][:], in1=qkv_st[_p2][:],
                            op=ALU.mult))
                        pending_norm[par2] = dve_mile()
                    else:
                        p.wait("te", a_act, ta_e)
                        p.wait("te", a_act, ps_vt_free[0])
                        for s4 in range(4):
                            last = s4 == 3
                            if last:
                                a_pe.tgt(1)
                                p.te(lambda e, _p2=par2, _s4=s4: e.transpose(
                                    ps_mT[:, _s4 * 128:(_s4 + 1) * 128],
                                    qkv_st[_p2][:, _s4 * 128:(_s4 + 1) * 128],
                                    idb[:]).then_inc(a_pe.h, 1))
                            else:
                                p.te(lambda e, _p2=par2, _s4=s4: e.transpose(
                                    ps_mT[:, _s4 * 128:(_s4 + 1) * 128],
                                    qkv_st[_p2][:, _s4 * 128:(_s4 + 1) * 128],
                                    idb[:]))
                        t_vt = a_pe.n
                        p.wait("act", a_pe, t_vt)
                        p.wait("act", qk_st, vt_free_t[par2])
                        p.act(lambda e, _p2=par2: e.copy(
                            out=vt_st[_p2][:], in_=ps_mT[:, :]))
                        ta_v = act_mile()
                        ps_vt_free[0] = ta_v
                        p.wait("sp", a_act, ta_v)
                        vt_free_t[par2] = dma(
                            "sp",
                            s_v[oc][:, j * 4:(j + 1) * 4, :],
                            vt_st[par2][:].rearrange("p (c f) -> p c f", c=4),
                            qk_st)
                        if oc == 0:
                            v0_st[0] = vt_free_t[par2]
                        qkv_free_t[par2] = 0

            p.wait("te", g_ld, t_idb)
            p.wait("te", g_dve, t_bias)
            p.wait("act", g_dve, t_bias)
            emit_xb_load(0)
            emit_squares(0, 0)
            emit_squares(0, 1)
            for c in range(DC):
                emit_sum_pass(0, c)
            for c in range(8):
                emit_sq_pass(0, c)
            emit_squares(0, 2)
            emit_squares(0, 3)
            for c in range(8, DC):
                emit_sq_pass(0, c)
            emit_rows(0)
            emit_bcast(0)
            for qtr in range(4):
                emit_norm_quarter(0, qtr)
            emit_spill(0)
            for s in range(4):
                emit_router_sub(0, s)

            for dc in range(DC):
                dma("pool", wslotA[:, dc * D:(dc + 1) * D], wq[dc * 128:(dc + 1) * 128, :], w_ld)
            t_wq = w_ld.n
            t_wk_holder = [0]

            def qhook_factory(j):
                t = j + 1
                base_map = {5: 0, 6: 2, 8: 4, 10: 6, 12: 8, 13: 10, 14: 12, 15: 14}

                def hook(oc):
                    if j == 0 and oc == 1:
                        for dc in range(DC):
                            dma("pool", wslotB[:, dc * D:(dc + 1) * D],
                                wk[dc * 128:(dc + 1) * 128, :], w_ld)
                        t_wk_holder[0] = w_ld.n
                    if t >= NTILE:
                        return
                    if oc == 0:
                        emit_xb_load(t)
                    elif oc == 3:
                        emit_squares(t, 0)
                    elif oc == 4:
                        emit_squares(t, 1)
                    elif oc == 7:
                        emit_squares(t, 2)
                    elif oc == 11:
                        emit_squares(t, 3)
                    elif oc in base_map:
                        c = base_map[oc]
                        emit_sum_pass(t, c)
                        emit_sum_pass(t, c + 1)
                        emit_sq_pass(t, c)
                        emit_sq_pass(t, c + 1)
                return hook

            def khook_factory(j):
                t = j + 1

                def hook(oc):
                    if t >= NTILE:
                        return
                    if oc == 0:
                        emit_rows(t)
                    elif oc == 1:
                        emit_bcast(t)
                    elif oc in (3, 4, 5, 6):
                        emit_norm_quarter(t, oc - 3)
                    elif oc == 7:
                        emit_spill(t)
                    elif oc in (9, 11, 13, 15):
                        emit_router_sub(t, (oc - 9) // 2)
                return hook

            t_q_done = 0
            for j in range(NTILE):
                px = j % 2
                p.wait("te", a_dve, t_norm_done[j])
                p.wait("te", w_ld, t_wq)
                emit_qkv(j, px, "q", wslotA, qhook_factory(j))
                if j == NTILE - 1:
                    t_q_done = a_pe.n
                p.wait("te", w_ld, t_wk_holder[0])
                emit_qkv(j, px, "k", wslotB, khook_factory(j))
                xnf_free_t[px] = a_pe.n
            flush_norm(0, 0)
            flush_norm(1, 1)

            p.dve(lambda e: e.tensor_scalar(
                out=negrow[0:1, :], in0=qknorm[0:1, 0:1], scalar1=qknorm[0:1, 1:2],
                scalar2=-0.5, op0=ALU.add, op1=ALU.mult))
            td_nb = dve_mile()
            p.wait("te", a_dve, td_nb)
            tp_nb = a_pe.tgt(1)
            p.te(lambda e: e.matmul(ps_r[:, 0:1], ones1_f[0:1, :], negrow[0:1, :],
                                    start=True, stop=True).then_inc(a_pe.h, 1))
            p.wait("act", a_pe, tp_nb)
            p.act(lambda e: e.copy(out=negb[:], in_=ps_r[:, 0:1]))
            t_negb_a = act_mile()
            p.act(lambda e: e.activation(out=negrow[0:1, :], in_=negrow[0:1, :],
                                         func=AF.Exp, scale=0.0))

            p.wait("pool", a_pe, t_q_done)
            for dc in range(DC):
                dma("pool", wslotA[:, dc * D:(dc + 1) * D], wv[dc * 128:(dc + 1) * 128, :], w_ld)
            t_wv = w_ld.n
            p.wait("te", w_ld, t_wv)
            ps_vt_free[0] = st_maskT_evac[0]
            for j in range(NTILE):
                px = j % 2
                p.wait("pool", a_pe, xnf_free_t[px])
                p.wait("pool", xn_sp, xn_spill_tile[j])
                t_r = 0
                for half in range(2):
                    t_r = dma("pool", xnf[px][:].rearrange("p (c t) -> p c t", c=DC)[:, half * 8:(half + 1) * 8, :],
                              s_xn.rearrange("(c p) t -> p c t", p=128)[:, half * 8:(half + 1) * 8,
                                                                       j * TILE:(j + 1) * TILE],
                              xn_rld[px])
                p.wait("te", xn_rld[px], t_r)
                emit_qkv(j, px, "v", wslotA)
                xnf_free_t[px] = a_pe.n

            p.wait("sp", qk_st, qk_st.n)
            p.wait("sp", xn_sp, xn_sp.n)
            p.sp(lambda e: e.nop().then_inc(phase.h, 1))
            p.te(lambda e: e.nop().then_inc(phase.h, 1))
            p.act(lambda e: e.nop().then_inc(phase.h, 1))
            p.dve(lambda e: e.nop().then_inc(phase.h, 1))
            p.pool(lambda e: e.nop().then_inc(phase.h, 1))
            phase.n += 5
            t_phase_a = phase.n

        b_pe = mksem("b_pe")
        b_act = mksem("b_act")
        b_dve = mksem("b_dve")
        wo_ld = mksem("wo_ld")
        q_ld = [mksem("q_ld0"), mksem("q_ld1"), mksem("q_ld2"), mksem("q_ld3")]
        at_st = mksem("at_st")
        at_ld = [mksem("at_ld0"), mksem("at_ld1")]
        ao_st = mksem("ao_st")

        with ExitStack() as bctx:
            bsb = lambda n, s, d: bctx.enter_context(nc.sbuf_tensor(n, s, d))
            bps = lambda n, s, d: bctx.enter_context(nc.psum_tensor(n, s, d))
            ksb = [bsb("ksb0", [128, T], BF16), bsb("ksb1", [128, T], BF16)]
            vsb = [bsb("vsb0", [128, T], BF16), bsb("vsb1", [128, T], BF16)]
            qsb = [bsb("qsb0", [128, TILE], BF16), bsb("qsb1", [128, TILE], BF16),
                   bsb("qsb2", [128, TILE], BF16), bsb("qsb3", [128, TILE], BF16)]
            o_tmp = bsb("o_tmp", [128, TILE], F32)
            rzb_sb = bsb("rzb_sb", [128, TILE], F32)
            zsum = bsb("zsum", [128, 4 * TILE], BF16)
            ones_sq = bsb("ones_sq", [128, 128], BF16)
            cbT = bsb("cbT", [128, 4 * TILE], BF16)
            pTs = bsb("pTs", [128, 4 * T], BF16)
            oout = [bsb("oout0", [128, TILE], BF16), bsb("oout1", [128, TILE], BF16)]
            woB = bsb("woB", [128, DC * D], BF16)
            atb = [bsb("atb0", [128, DC * TILE], BF16), bsb("atb1", [128, DC * TILE], BF16)]
            aost = [bsb("aost0", [128, TILE], F32), bsb("aost1", [128, TILE], F32)]
            jdve = bsb("jdve", [128, 1], F32)
            jact = bsb("jact", [128, 1], F32)

            ps_sT = [bps("ps_sT0", [128, 1024], F32), bps("ps_sT1", [128, 1024], F32),
                     bps("ps_sT2", [128, 1024], F32)]
            ps_o = bps("ps_o", [128, TILE], F32)
            ps_z = bps("ps_z", [128, TILE], F32)

            def bdve_mile():
                t = b_dve.tgt(1)
                p.dve(lambda e: e.tensor_copy(out=jdve[:], in_=epsc[:]).then_inc(b_dve.h, 1))
                return t

            def bdve_fence():
                t = bdve_mile()
                p.wait("dve", b_dve, t)

            def bact_mile():
                t = b_act.tgt(1)
                p.act(lambda e: e.copy(out=jact[:], in_=epsc[:]).then_inc(b_act.h, 1))
                return t

            for key in Prog.KEYS:
                p.wait(key, phase, t_phase_a)
            p.dve(lambda e: e.memset(ones_sq[:], 1.0))

            t_cb = dma("pool", cbT[:], causal[:, :], wo_ld)
            p.wait("dve", wo_ld, t_cb)
            cast_pieces = []
            for i4 in range(16):
                cast_pieces.append(lambda _i=i4: dma(
                    "pool", s_w1[_i * 128:(_i + 1) * 128, :], w1t[_i * 128:(_i + 1) * 128, :],
                    s_w1_ld))
            for i4 in range(16):
                cast_pieces.append(lambda _i=i4: dma("pool", s_w2[_i], w2p[_i], s_w2_ld))

            kv_free_pe = [0, 0]
            q_free_pe = [0, 0, 0, 0]
            oout_free = [0, 0]
            ps_sT_free = [0, 0, 0]
            ps_o_free = 0
            z_free = 0
            o_used = [0]
            zpass_pe = [0, 0, 0, 0]

            p.wait("act", a_act, t_negb_a)

            t_kh = [0] * NH
            t_vh = [0] * NH

            def load_kv(h):
                hp_ = h % 2
                p.wait("pool", b_pe, kv_free_pe[hp_])
                t_kh[h] = dma("pool", ksb[hp_][:], s_k[h * 128:(h + 1) * 128, :], k_ld[hp_])
                t_vh[h] = dma("pool", vsb[hp_][:].rearrange("p (c f) -> p c f", c=T // 128),
                              s_v[h], v_ld[hp_])

            load_kv(0)
            for dc in range(DC):
                dma("pool", woB[:, dc * D:(dc + 1) * D], wo[dc * 128:(dc + 1) * 128, :], wo_ld)
            t_wo = wo_ld.n
            for h in range(NH):
                hp = h % 2
                if h + 1 < NH:
                    load_kv(h + 1)
                t_k, t_v = t_kh[h], t_vh[h]

                for i in range(NTILE):
                    qp = (h * NTILE + i) % 4
                    p.wait("sp", b_pe, q_free_pe[qp])
                    t_q = dma("sp", qsb[qp][:],
                              s_q[h * 128:(h + 1) * 128, i * TILE:(i + 1) * TILE], q_ld[qp])
                    if cast_pieces:
                        cast_pieces.pop(0)()
                    p.wait("te", q_ld[qp], t_q)

                    nkc = 4 * (i + 1)
                    npair = nkc // 2
                    pend = []
                    zadd_t = {}
                    zadd_pend = []

                    def emit_zadd(pr, ws_, wt_):
                        zsl = pr % 4
                        p.wait("dve", ws_, wt_)
                        p.wait("dve", b_pe, zpass_pe[zsl])
                        p.dve(lambda e, _pr=pr, _z=zsl: e.tensor_tensor(
                            out=zsum[:, _z * TILE:(_z + 1) * TILE],
                            in0=pTs[:, _pr * 1024:_pr * 1024 + TILE],
                            in1=pTs[:, _pr * 1024 + TILE:(_pr + 1) * 1024],
                            op=ALU.add))
                        zadd_t[pr] = bdve_mile()

                    def emit_zo(pd):
                        pr, ws_, wt_ = pd
                        zsl = pr % 4
                        p.wait("te", ws_, wt_)
                        p.wait("te", b_dve, zadd_t[pr])
                        if pr == 0:
                            p.wait("te", b_act, ps_o_free)
                            p.wait("te", b_dve, z_free)
                            p.wait("te", v_ld[hp], t_v)
                        zpass_pe[zsl] = b_pe.tgt(1)
                        p.te(lambda e, _z=zsl, _st=pr == 0, _sp=pr == npair - 1: e.matmul(
                            ps_z[:, :], ones_sq[:, :],
                            zsum[:, _z * TILE:(_z + 1) * TILE],
                            start=_st, stop=_sp).then_inc(b_pe.h, 1))
                        for kidx, kc in enumerate((2 * pr, 2 * pr + 1)):
                            st_ = pr == 0 and kidx == 0
                            sp_ = pr == npair - 1 and kidx == 1
                            if sp_:
                                b_pe.tgt(1)
                                p.te(lambda e, _hp=hp, _kc=kc, _st=st_, _sp=sp_: e.matmul(
                                    ps_o[:], vsb[_hp][:, _kc * 128:(_kc + 1) * 128],
                                    pTs[:, _kc * 512:(_kc + 1) * 512],
                                    start=_st, stop=_sp).then_inc(b_pe.h, 1))
                            else:
                                p.te(lambda e, _hp=hp, _kc=kc, _st=st_, _sp=sp_: e.matmul(
                                    ps_o[:], vsb[_hp][:, _kc * 128:(_kc + 1) * 128],
                                    pTs[:, _kc * 512:(_kc + 1) * 512],
                                    start=_st, stop=_sp))

                    for pair in range(npair):
                        while zadd_pend and zadd_pend[0][0] <= pair - 1:
                            emit_zadd(*zadd_pend.pop(0))
                        slot = pair % 3
                        kc0, kc1 = 2 * pair, 2 * pair + 1
                        p.wait("te", b_act, ps_sT_free[slot])
                        p.wait("te", k_ld[hp], t_k)
                        for kidx, kc in enumerate((kc0, kc1)):
                            last = kidx == 1
                            if last:
                                tp_s = b_pe.tgt(1)
                                p.te(lambda e, _hp=hp, _qp=qp, _kc=kc, _sl=slot, _ki=kidx: e.matmul(
                                    ps_sT[_sl][:, _ki * 512:(_ki + 1) * 512],
                                    ksb[_hp][:, _kc * 128:(_kc + 1) * 128],
                                    qsb[_qp][:], start=True, stop=True).then_inc(b_pe.h, 1))
                            else:
                                p.te(lambda e, _hp=hp, _qp=qp, _kc=kc, _sl=slot, _ki=kidx: e.matmul(
                                    ps_sT[_sl][:, _ki * 512:(_ki + 1) * 512],
                                    ksb[_hp][:, _kc * 128:(_kc + 1) * 128],
                                    qsb[_qp][:], start=True, stop=True))
                        p.wait("act", b_pe, tp_s)
                        if pair >= 2 * i:
                            p.act(lambda e, _sl=slot, _pr=pair: e.activation(
                                out=pTs[:, _pr * 1024:(_pr + 1) * 1024],
                                in_=ps_sT[_sl][:], func=AF.Exp, bias=negb[:, 0:1]))
                            ps_sT_free[slot] = bact_mile()
                        else:
                            ps_sT_free[slot] = b_act.tgt(1)
                            p.act(lambda e, _sl=slot, _pr=pair: e.activation(
                                out=pTs[:, _pr * 1024:(_pr + 1) * 1024],
                                in_=ps_sT[_sl][:], func=AF.Exp,
                                bias=negb[:, 0:1]).then_inc(b_act.h, 1))
                        if pair >= 2 * i:
                            s0 = (pair - 2 * i) * 2
                            p.wait("dve", b_act, ps_sT_free[slot])
                            p.dve(lambda e, _pr=pair, _s0=s0: e.tensor_tensor(
                                out=pTs[:, _pr * 1024:(_pr + 1) * 1024],
                                in0=pTs[:, _pr * 1024:(_pr + 1) * 1024],
                                in1=cbT[:, _s0 * 512:(_s0 + 2) * 512], op=ALU.mult))
                            tdm = bdve_mile()
                            pend.append((pair, b_dve, tdm))
                            emit_zadd(pair, b_dve, tdm)
                        else:
                            pend.append((pair, b_act, ps_sT_free[slot]))
                            zadd_pend.append((pair, b_act, ps_sT_free[slot]))
                        if len(pend) > 2:
                            emit_zo(pend.pop(0))
                    while zadd_pend:
                        emit_zadd(*zadd_pend.pop(0))
                    while pend:
                        emit_zo(pend.pop(0))
                    t_scores_done = b_pe.n

                    op_ = (h * NTILE + i) % 2
                    p.wait("act", b_pe, t_scores_done)
                    p.wait("act", b_dve, o_used[0])
                    p.act(lambda e: e.copy(out=o_tmp[:], in_=ps_o[:]))
                    ps_o_free = bact_mile()
                    p.wait("dve", b_pe, t_scores_done)
                    p.dve(lambda e: e.reciprocal(rzb_sb[:], ps_z[:, :]))
                    z_free = bdve_mile()
                    p.wait("dve", b_dve, z_free)
                    p.wait("dve", b_act, ps_o_free)
                    p.wait("dve", at_st, oout_free[op_])
                    p.dve(lambda e, _op=op_: e.tensor_tensor(
                        out=oout[_op][:], in0=o_tmp[:], in1=rzb_sb[:], op=ALU.mult))
                    o_used[0] = bdve_mile()
                    p.wait("pool", b_dve, o_used[0])
                    oout_free[op_] = dma(
                        "pool", s_attn[h * 128:(h + 1) * 128, i * TILE:(i + 1) * TILE],
                        oout[op_][:], at_st)
                    if h == NH - 1 and i == 0:
                        p.wait("pool", at_st, at_st.n)
                        for half in range(2):
                            t_at_pre = dma(
                                "pool",
                                atb[0][:].rearrange("p (c t) -> p c t", c=DC)[:, half * 8:(half + 1) * 8, :],
                                s_attn.rearrange("(c p) t -> p c t", p=128)[:, half * 8:(half + 1) * 8,
                                                                           0:TILE],
                                at_ld[0])
                    q_free_pe[qp] = t_scores_done
                kv_free_pe[hp] = b_pe.n

            t_w1cast = s_w1_ld.n
            t_w2cast = s_w2_ld.n
            t_b1_st = at_st.n

            def load_atb(j, gate):
                ap_ = j % 2
                p.wait("pool", at_st, gate)
                p.wait("pool", b_pe, atb_free[ap_])
                t_at = 0
                for half in range(2):
                    t_at = dma("pool",
                               atb[ap_][:].rearrange("p (c t) -> p c t", c=DC)[:, half * 8:(half + 1) * 8, :],
                               s_attn.rearrange("(c p) t -> p c t", p=128)[:, half * 8:(half + 1) * 8,
                                                                          j * TILE:(j + 1) * TILE],
                               at_ld[ap_])
                return t_at

            atb_free = [0, 0]
            aost_free = [0, 0]
            ps_b2_free = list(ps_sT_free)
            p.wait("sp", at_st, t_b1_st)
            p.wait("te", wo_ld, t_wo)
            for j in range(NTILE):
                ap_ = j % 2
                t_at = t_at_pre if j == 0 else load_atb(j, t_b1_st)
                p.wait("te", at_ld[ap_], t_at)
                for dm in range(DC):
                    par = dm % 2
                    p.wait("te", b_act, ps_b2_free[par])
                    for hfc in range(DC):
                        last = hfc == DC - 1
                        if last:
                            tp_b2 = b_pe.tgt(1)
                            p.te(lambda e, _a=ap_, _dm=dm, _h=hfc, _p=par: e.matmul(
                                ps_sT[_p][:, 0:512],
                                woB[:, _h * D + _dm * 128: _h * D + (_dm + 1) * 128],
                                atb[_a][:, _h * TILE:(_h + 1) * TILE],
                                start=(_h == 0), stop=True).then_inc(b_pe.h, 1))
                        else:
                            p.te(lambda e, _a=ap_, _dm=dm, _h=hfc, _p=par: e.matmul(
                                ps_sT[_p][:, 0:512],
                                woB[:, _h * D + _dm * 128: _h * D + (_dm + 1) * 128],
                                atb[_a][:, _h * TILE:(_h + 1) * TILE],
                                start=(_h == 0), stop=False))
                    p.wait("act", b_pe, tp_b2)
                    p.wait("act", ao_st, aost_free[par])
                    p.act(lambda e, _p=par, _dm=dm: e.activation(
                        out=aost[_p][:], in_=ps_sT[_p][:, 0:512], func=AF.Identity,
                        bias=bos[:, _dm:_dm + 1]))
                    ps_b2_free[par] = bact_mile()
                    p.wait("sp", b_act, ps_b2_free[par])
                    aost_free[par] = dma(
                        "sp", s_ao[dm * 128:(dm + 1) * 128, j * TILE:(j + 1) * TILE],
                        aost[par][:], ao_st)
                atb_free[ap_] = b_pe.n

            p.wait("sp", ao_st, ao_st.n)
            p.sp(lambda e: e.nop().then_inc(phase.h, 1))
            p.te(lambda e: e.nop().then_inc(phase.h, 1))
            p.act(lambda e: e.nop().then_inc(phase.h, 1))
            p.dve(lambda e: e.nop().then_inc(phase.h, 1))
            p.pool(lambda e: e.nop().then_inc(phase.h, 1))
            phase.n += 5
            t_phase_b = phase.n

        c_pe = mksem("c_pe")
        c_act = mksem("c_act")
        c_dve = mksem("c_dve")
        xnb_ld = [mksem("xnb_ld0"), mksem("xnb_ld1")]
        w1_ld = [mksem("w1_ld0"), mksem("w1_ld1")]
        w2_ld = [mksem("w2_ld0"), mksem("w2_ld1")]
        xs_ld = [mksem("xs_ld0"), mksem("xs_ld1")]
        y_st = mksem("y_st")

        with ExitStack() as cctx:
            csb = lambda n, s, d: cctx.enter_context(nc.sbuf_tensor(n, s, d))
            cps = lambda n, s, d: cctx.enter_context(nc.psum_tensor(n, s, d))
            xnc = [csb("xnc0", [128, DC * TILE], BF16), csb("xnc1", [128, DC * TILE], BF16)]
            h_sb = csb("h_sb", [128, FC * TILE], BF16)
            w1g = [csb("w1g0", [128, DC * TILE], BF16), csb("w1g1", [128, DC * TILE], BF16)]
            w2c = [csb("w2c0", [128, FC * 128], BF16), csb("w2c1", [128, FC * 128], BF16)]
            mbb = [csb("mbb0", [128, TILE], BF16), csb("mbb1", [128, TILE], BF16)]
            gel = [csb("gel0", [128, TILE], BF16), csb("gel1", [128, TILE], BF16)]
            xsl = [csb("xsl0", [128, TILE], F32), csb("xsl1", [128, TILE], F32)]
            aosl = [csb("aosl0", [128, TILE], F32), csb("aosl1", [128, TILE], F32)]
            accb = [csb("accb0", [128, TILE], F32), csb("accb1", [128, TILE], F32)]
            yst_b = [csb("yst0", [128, TILE], F32), csb("yst1", [128, TILE], F32)]
            jd2 = csb("jd2", [128, 1], F32)
            ja2 = csb("ja2", [128, 1], F32)

            ps_h = [cps("ps_h0", [128, TILE], F32), cps("ps_h1", [128, TILE], F32)]
            ps_mb = cps("ps_mb", [128, TILE], F32)
            ps_y = [cps("ps_y0", [128, TILE], F32), cps("ps_y1", [128, TILE], F32)]

            def cdve_mile():
                t = c_dve.tgt(1)
                p.dve(lambda e: e.tensor_copy(out=jd2[:], in_=epsc[:]).then_inc(c_dve.h, 1))
                return t

            def cact_mile():
                t = c_act.tgt(1)
                p.act(lambda e: e.copy(out=ja2[:], in_=epsc[:]).then_inc(c_act.h, 1))
                return t

            for key in Prog.KEYS:
                p.wait(key, phase, t_phase_b)
            p.wait("sp", s_w1_ld, t_w1cast)
            p.wait("sp", s_w2_ld, t_w2cast)

            xnc_free = [0, 0]
            w1_free = [0, 0]
            w2_free = [0, 0]
            mbb_free = [0, 0]
            gel_free = [0, 0]
            ps_h_free = [0, 0]
            ps_mb_free = 0
            ps_y_free = [0, 0]
            xs_free = [0, 0]
            yst_free = [0, 0]
            h_free_pe = 0

            for j in range(NTILE):
                jp = j % 2
                p.wait("sp", c_pe, xnc_free[jp])
                t_xn = 0
                for half in range(2):
                    t_xn = dma("sp",
                               xnc[jp][:].rearrange("p (c t) -> p c t", c=DC)[:, half * 8:(half + 1) * 8, :],
                               s_xn.rearrange("(c p) t -> p c t", p=128)[:, half * 8:(half + 1) * 8,
                                                                        j * TILE:(j + 1) * TILE],
                               xnb_ld[jp])
                for g in range(NB):
                    wp = g % 2
                    p.wait("sp", c_pe, w1_free[wp])
                    t_w1 = 0
                    for half in range(2):
                        t_w1 = dma("sp",
                                   w1g[wp][:].rearrange("p (c f) -> p c f", c=DC)[:, half * 8:(half + 1) * 8, :],
                                   s_w1.rearrange("(c p) f -> p c f", p=128)[:, half * 8:(half + 1) * 8,
                                                                            g * TILE:(g + 1) * TILE],
                                   w1_ld[wp])
                    p.wait("te", c_act, ps_mb_free)
                    p.wait("te", c_dve, mbb_free[wp])
                    tp_mb = c_pe.tgt(1)
                    p.te(lambda e, _g=g, _j=j: e.matmul(
                        ps_mb[:], selb[0:16, _g * 128:(_g + 1) * 128],
                        mask_fm[0:16, _j * TILE:(_j + 1) * TILE],
                        start=True, stop=True).then_inc(c_pe.h, 1))
                    p.wait("act", c_pe, tp_mb)
                    p.act(lambda e, _wp=wp: e.copy(out=mbb[_wp][:], in_=ps_mb[:]))
                    ps_mb_free = cact_mile()
                    t_mb = ps_mb_free
                    for fc4 in range(4):
                        fc = 4 * g + fc4
                        hp2 = fc % 2
                        p.wait("te", w1_ld[wp], t_w1)
                        p.wait("te", xnb_ld[jp], t_xn)
                        p.wait("te", c_act, ps_h_free[hp2])
                        for dc in range(DC):
                            last = dc == DC - 1
                            if last:
                                tp_h = c_pe.tgt(1)
                                p.te(lambda e, _wp=wp, _f=fc4, _dc=dc, _jp=jp, _h2=hp2: e.matmul(
                                    ps_h[_h2][:],
                                    w1g[_wp][:, _dc * TILE + _f * 128: _dc * TILE + (_f + 1) * 128],
                                    xnc[_jp][:, _dc * TILE:(_dc + 1) * TILE],
                                    start=(_dc == 0), stop=True).then_inc(c_pe.h, 1))
                            else:
                                p.te(lambda e, _wp=wp, _f=fc4, _dc=dc, _jp=jp, _h2=hp2: e.matmul(
                                    ps_h[_h2][:],
                                    w1g[_wp][:, _dc * TILE + _f * 128: _dc * TILE + (_f + 1) * 128],
                                    xnc[_jp][:, _dc * TILE:(_dc + 1) * TILE],
                                    start=(_dc == 0), stop=False))
                        p.wait("act", c_pe, tp_h)
                        p.wait("act", c_dve, gel_free[hp2])
                        p.act(lambda e, _h2=hp2: e.activation(
                            out=gel[_h2][:], in_=ps_h[_h2][:], func=AF.Gelu))
                        ps_h_free[hp2] = cact_mile()
                        p.wait("dve", c_act, ps_h_free[hp2])
                        p.wait("dve", c_act, t_mb)
                        p.wait("dve", c_pe, h_free_pe)
                        p.dve(lambda e, _h2=hp2, _wp=wp, _fc=fc: e.tensor_tensor(
                            out=h_sb[:, _fc * TILE:(_fc + 1) * TILE],
                            in0=gel[_h2][:], in1=mbb[_wp][:], op=ALU.mult))
                        tmil = cdve_mile()
                        gel_free[hp2] = tmil
                        if fc4 == 3:
                            mbb_free[wp] = tmil
                    w1_free[wp] = c_pe.n
                t_h_done = c_dve.n

                for dm in range(DC):
                    wp2 = dm % 2
                    p.wait("sp", c_pe, w2_free[wp2])
                    t_w2 = dma("sp", w2c[wp2][:], s_w2[dm], w2_ld[wp2])
                    p.wait("pool", c_dve, xs_free[wp2])
                    dma("pool", xsl[wp2][:],
                        x_fm[dm * 128:(dm + 1) * 128, j * TILE:(j + 1) * TILE], xs_ld[wp2])
                    t_xs = dma("pool", aosl[wp2][:],
                               s_ao[dm * 128:(dm + 1) * 128, j * TILE:(j + 1) * TILE], xs_ld[wp2])
                    p.wait("te", w2_ld[wp2], t_w2)
                    p.wait("te", c_dve, t_h_done)
                    p.wait("te", c_dve, ps_y_free[wp2])
                    for fc in range(FC):
                        last = fc == FC - 1
                        if last:
                            tp_y = c_pe.tgt(1)
                            p.te(lambda e, _w=wp2, _fc=fc: e.matmul(
                                ps_y[_w][:], w2c[_w][:, _fc * 128:(_fc + 1) * 128],
                                h_sb[:, _fc * TILE:(_fc + 1) * TILE],
                                start=(_fc == 0), stop=True).then_inc(c_pe.h, 1))
                        else:
                            p.te(lambda e, _w=wp2, _fc=fc: e.matmul(
                                ps_y[_w][:], w2c[_w][:, _fc * 128:(_fc + 1) * 128],
                                h_sb[:, _fc * TILE:(_fc + 1) * TILE],
                                start=(_fc == 0), stop=False))
                    w2_free[wp2] = c_pe.n
                    p.wait("dve", xs_ld[wp2], t_xs)
                    p.dve(lambda e, _w=wp2: e.tensor_tensor(
                        out=accb[_w][:], in0=xsl[_w][:], in1=aosl[_w][:], op=ALU.add))
                    xs_free[wp2] = cdve_mile()
                    p.wait("dve", c_dve, xs_free[wp2])
                    p.wait("dve", c_pe, tp_y)
                    p.wait("dve", y_st, yst_free[wp2])
                    p.dve(lambda e, _w=wp2: e.tensor_tensor(
                        out=yst_b[_w][:], in0=ps_y[_w][:], in1=accb[_w][:], op=ALU.add))
                    ps_y_free[wp2] = cdve_mile()
                    p.wait("pool", c_dve, ps_y_free[wp2])
                    yst_free[wp2] = dma(
                        "pool", y[dm * 128:(dm + 1) * 128, j * TILE:(j + 1) * TILE],
                        yst_b[wp2][:], y_st)
                h_free_pe = c_pe.n
                xnc_free[jp] = c_pe.n

            p.wait("pool", y_st, y_st.n)
            p.wait("sp", y_st, y_st.n)

        with nc.Block() as blk:
            @blk.tensor
            def _(e):
                for f in p.q["te"]:
                    f(e)

            @blk.scalar
            def _(e):
                for f in p.q["act"]:
                    f(e)

            @blk.vector
            def _(e):
                for f in p.q["dve"]:
                    f(e)

            @blk.gpsimd
            def _(e):
                for f in p.q["pool"]:
                    f(e)

            @blk.sync
            def _(e):
                for f in p.q["sp"]:
                    f(e)

    return nc


def _host_prep(inputs):
    x = np.ascontiguousarray(inputs["L0_attn_query"])
    shared = {
        "wq": np.ascontiguousarray(inputs["L0_attn_w_q"]),
        "wk": np.ascontiguousarray(inputs["L0_attn_w_k"]),
        "wv": np.ascontiguousarray(inputs["L0_attn_w_v"]),
        "wo": np.ascontiguousarray(inputs["L0_attn_w_o"]),
        "w1t": np.ascontiguousarray(inputs["L0_ffn_W1"].T),
        "w2p": np.ascontiguousarray(
            inputs["L0_ffn_W2"].reshape(DC, 128, FC, 128).transpose(0, 3, 2, 1)
            .reshape(DC, 128, FC * 128)),
        "wrt": np.ascontiguousarray(inputs["L0_ffn_Wr"].T),
        "bq2": np.ascontiguousarray(inputs["L0_attn_b_q"].reshape(16, 128).T),
        "bk2": np.ascontiguousarray(inputs["L0_attn_b_k"].reshape(16, 128).T),
        "bv2": np.ascontiguousarray(inputs["L0_attn_b_v"].reshape(16, 128).T),
        "bo2": np.ascontiguousarray(inputs["L0_attn_b_o"].reshape(16, 128).T),
        "brr": np.ascontiguousarray(inputs["L0_ffn_br"].reshape(1, NB)),
    }
    cbias = np.zeros((128, 4, TILE), dtype=np.float32)
    pp = np.arange(128)[:, None]
    ff = np.arange(TILE)[None, :]
    for s in range(4):
        cbias[:, s, :] = (ff >= s * 128 + pp).astype(np.float32)
    shared["causal"] = np.ascontiguousarray(cbias.reshape(128, 4 * TILE))
    shared["identf"] = np.eye(128, dtype=np.float32)
    selm = np.zeros((NB, NB * 128), dtype=np.float32)
    for e_ in range(NB):
        selm[e_, e_ * 128:(e_ + 1) * 128] = 1.0
    shared["self_"] = selm
    per_core = []
    for c in range(8):
        m = dict(shared)
        m["x_fm"] = np.ascontiguousarray(x[c].T)
        per_core.append(m)
    return per_core


def kernel(**inputs):
    per_core = _host_prep(inputs)
    nc = build(debug=False)
    res = run_bass_kernel_spmd(nc, per_core, core_ids=list(range(8)))
    out = np.stack([np.ascontiguousarray(r["y"].T) for r in res.results], axis=0)
    return out


# version 16
# speedup vs baseline: 1.0469x; 1.0469x over previous
import numpy as np
from contextlib import ExitStack

import concourse.bass as bass
import concourse.mybir as mybir
from concourse.bass_utils import run_bass_kernel_spmd

F32 = mybir.dt.float32
BF16 = mybir.dt.bfloat16
AF = mybir.ActivationFunctionType
ALU = mybir.AluOpType
AX = mybir.AxisListType

T = 4096
D = 2048
NH = 16
DH = 128
FF = 8192
NB = 16
BS = 512
TILE = 512
NTILE = T // TILE
DC = D // 128
FC = FF // 128
ALPHA = float(128.0 ** -0.25)
EPS = 1e-5


class Sem:

    def __init__(self, handle):
        self.h = handle
        self.n = 0

    def tgt(self, k):
        self.n += k
        return self.n


class Prog:

    KEYS = ("te", "act", "dve", "pool", "sp")

    def __init__(self):
        self.q = {k: [] for k in self.KEYS}
        self.hw = {}

    def te(self, f):
        self.q["te"].append(f)

    def act(self, f):
        self.q["act"].append(f)

    def dve(self, f):
        self.q["dve"].append(f)

    def pool(self, f):
        self.q["pool"].append(f)

    def sp(self, f):
        self.q["sp"].append(f)

    def wait(self, key, sem, n):
        if n <= 0 or self.hw.get((key, sem.h.num), -1) >= n:
            return
        self.hw[(key, sem.h.num)] = n
        self.q[key].append(lambda e, _h=sem.h, _n=n: e.wait_ge(_h, _n))


def build(debug=False):
    nc = bass.Bass("TRN2", target_bir_lowering=False)

    x_fm = nc.dram_tensor("x_fm", [D, T], F32, kind="ExternalInput").ap()
    wq = nc.dram_tensor("wq", [D, D], F32, kind="ExternalInput").ap()
    wk = nc.dram_tensor("wk", [D, D], F32, kind="ExternalInput").ap()
    wv = nc.dram_tensor("wv", [D, D], F32, kind="ExternalInput").ap()
    wo = nc.dram_tensor("wo", [D, D], F32, kind="ExternalInput").ap()
    w1t = nc.dram_tensor("w1t", [D, FF], F32, kind="ExternalInput").ap()
    w2p = nc.dram_tensor("w2p", [DC, 128, FC * 128], F32, kind="ExternalInput").ap()
    wrt = nc.dram_tensor("wrt", [D, NB], F32, kind="ExternalInput").ap()
    bq2 = nc.dram_tensor("bq2", [128, 16], F32, kind="ExternalInput").ap()
    bk2 = nc.dram_tensor("bk2", [128, 16], F32, kind="ExternalInput").ap()
    bv2 = nc.dram_tensor("bv2", [128, 16], F32, kind="ExternalInput").ap()
    bo2 = nc.dram_tensor("bo2", [128, 16], F32, kind="ExternalInput").ap()
    brr = nc.dram_tensor("brr", [1, NB], F32, kind="ExternalInput").ap()
    causal = nc.dram_tensor("causal", [128, 4 * TILE], F32, kind="ExternalInput").ap()
    identf = nc.dram_tensor("identf", [128, 128], F32, kind="ExternalInput").ap()
    self_ = nc.dram_tensor("self_", [NB, NB * 128], F32, kind="ExternalInput").ap()

    y = nc.dram_tensor("y", [D, T], F32, kind="ExternalOutput").ap()

    s_w1 = nc.dram_tensor("s_w1", [D, FF], BF16, kind="Internal").ap()
    s_w2 = nc.dram_tensor("s_w2", [DC, 128, FC * 128], BF16, kind="Internal").ap()
    s_xn = nc.dram_tensor("s_xn", [D, T], BF16, kind="Internal").ap()
    s_q = nc.dram_tensor("s_q", [D, T], BF16, kind="Internal").ap()
    s_k = nc.dram_tensor("s_k", [D, T], BF16, kind="Internal").ap()
    s_v = nc.dram_tensor("s_v", [NH, 128, T // 128, DH], BF16, kind="Internal").ap()
    s_attn = nc.dram_tensor("s_attn", [D, T], BF16, kind="Internal").ap()
    s_ao = nc.dram_tensor("s_ao", [D, T], F32, kind="Internal").ap()

    p = Prog()

    with ExitStack() as gctx:
        sbuf = lambda n, s, d: gctx.enter_context(nc.sbuf_tensor(n, s, d))
        mksem = lambda n: Sem(gctx.enter_context(nc.semaphore(n)))

        idb = sbuf("idb", [128, 128], BF16)
        selb = sbuf("selb", [NB, NB * 128], BF16)
        wrtb = sbuf("wrtb", [128, DC * NB], BF16)
        brr_bf = sbuf("brr_bf", [1, NB], BF16)
        ones1_bf = sbuf("ones1_bf", [1, 128], BF16)
        ones128_bf = sbuf("ones128_bf", [128, 1], BF16)
        ones1_f = sbuf("ones1_f", [1, 128], F32)
        epsc = sbuf("epsc", [128, 1], F32)
        bqs = sbuf("bqs", [128, 16], F32)
        bks = sbuf("bks", [128, 16], F32)
        bvs = sbuf("bvs", [128, 16], F32)
        bos = sbuf("bos", [128, 16], F32)
        bqsc = sbuf("bqsc", [128, 16], F32)
        bksc = sbuf("bksc", [128, 16], F32)
        mask_fm = sbuf("mask_fm", [NB, T], BF16)
        qknorm = sbuf("qknorm", [1, 2], F32)
        nrm2 = sbuf("nrm2", [1, 2], F32)
        nrm_row = sbuf("nrm_row", [1, TILE], F32)
        negrow = sbuf("negrow", [1, 1], F32)
        negb = sbuf("negb", [128, 1], F32)
        q2c = [sbuf("q2c0", [128, TILE], BF16), sbuf("q2c1", [128, TILE], BF16)]

        g_ld = mksem("g_ld")
        g_dve = mksem("g_dve")
        phase = mksem("phase")

        def dma(queue, out, in_, sem, k=16):
            t = sem.tgt(k)
            p.q[queue].append(lambda e, _o=out, _i=in_, _h=sem.h, _k=k:
                              e.dma_start(out=_o, in_=_i).then_inc(_h, _k))
            return t

        dma("pool", idb[:], identf[:, :], g_ld)
        t_idb = g_ld.n

        dma("sp", bqs[:], bq2[:, :], g_ld)
        dma("sp", bks[:], bk2[:, :], g_ld)
        dma("sp", bvs[:], bv2[:, :], g_ld)
        dma("sp", bos[:], bo2[:, :], g_ld)
        t_pro = g_ld.n

        def dve_op(f, sem=None):
            if sem is None:
                p.dve(f)
                return None
            t = sem.tgt(1)
            p.dve(lambda e, _f=f, _h=sem.h: _f(e).then_inc(_h, 1))
            return t

        p.wait("dve", g_ld, t_pro)
        dve_op(lambda e: e.memset(qknorm[:], 0.0))
        dve_op(lambda e: e.memset(ones1_bf[:], 1.0))
        dve_op(lambda e: e.memset(ones128_bf[:], 1.0))
        dve_op(lambda e: e.memset(ones1_f[:], 1.0))
        dve_op(lambda e: e.memset(epsc[:], EPS))
        dve_op(lambda e: e.tensor_scalar_mul(bqsc[:], bqs[:], ALPHA))
        t_bias = dve_op(lambda e: e.tensor_scalar_mul(bksc[:], bks[:], ALPHA), g_dve)

        s_w1_ld = mksem("s_w1_ld")
        s_w2_ld = mksem("s_w2_ld")

        w_ld = mksem("w_ld")
        xb_ld = [mksem("xb_ld0"), mksem("xb_ld1")]
        xn_sp = mksem("xn_sp")
        qk_st = mksem("qk_st")
        a_pe = mksem("a_pe")
        a_act = mksem("a_act")
        a_dve = mksem("a_dve")
        xn_rld = [mksem("xn_rld0"), mksem("xn_rld1")]
        k_ld = [mksem("k_ld0"), mksem("k_ld1")]
        v_ld = [mksem("v_ld0"), mksem("v_ld1")]

        with ExitStack() as actx:
            asb = lambda n, s, d: actx.enter_context(nc.sbuf_tensor(n, s, d))
            aps = lambda n, s, d: actx.enter_context(nc.psum_tensor(n, s, d))
            wslotA = asb("wslotA", [128, DC * D], BF16)
            wslotB = asb("wslotB", [128, DC * D], BF16)
            xnf = [asb("xnf0", [128, DC * TILE], BF16), asb("xnf1", [128, DC * TILE], BF16)]
            xsq = asb("xsq", [128, 8 * TILE], BF16)
            t1 = asb("t1", [128, 2 * TILE], BF16)
            mrs = asb("mrs", [128, 2 * TILE], BF16)
            rows = asb("rows", [1, 3 * TILE], F32)
            rows_bf = asb("rows_bf", [1, 2 * TILE], BF16)
            r_sb = asb("r_sb", [128, NB], F32)
            max8 = asb("max8", [128, 8], F32)
            mask_tm = asb("mask_tm", [128, NB], BF16)
            qkv_st = [asb("qkv_st0", [128, TILE], BF16), asb("qkv_st1", [128, TILE], BF16)]
            vt_st = [asb("vt_st0", [128, TILE], BF16), asb("vt_st1", [128, TILE], BF16)]
            jnkd = asb("jnkd", [128, 1], F32)
            jnka = asb("jnka", [128, 1], F32)

            ps_qkv = [aps("ps_qkv0", [128, TILE], F32), aps("ps_qkv1", [128, TILE], F32)]
            ps_r = aps("ps_r", [128, NB], F32)
            ps_mT = aps("ps_mT", [128, TILE], BF16)
            ps_bc = aps("ps_bc", [128, TILE], F32)
            ps_sum = aps("ps_sum", [1, TILE], F32)
            ps_sq = aps("ps_sq", [1, TILE], F32)
            ps_nrm = aps("ps_nrm", [1, TILE], F32)

            def dve_mile():
                t = a_dve.tgt(1)
                p.dve(lambda e: e.tensor_copy(out=jnkd[:], in_=epsc[:]).then_inc(a_dve.h, 1))
                return t

            def dve_fence():
                t = dve_mile()
                p.wait("dve", a_dve, t)

            def act_mile():
                t = a_act.tgt(1)
                p.act(lambda e: e.copy(out=jnka[:], in_=epsc[:]).then_inc(a_act.h, 1))
                return t

            dma("pool", selb[:], self_[:, :], g_ld)
            dma("pool", brr_bf[:], brr[:, :], g_ld)
            for dc in range(DC):
                dma("pool", wrtb[:, dc * NB:(dc + 1) * NB],
                    wrt[dc * 128:(dc + 1) * 128, :], g_ld)
            t_pro_pool = g_ld.n

            xb_half = {}
            sq_mile = {}
            sq_pass_pe = {}
            t_statdone = {}
            t_rows_free = {-1: 0}
            t_rstd_dve = {}
            t_norm_done = {0: 0}
            t_mrs_ok = {}
            st_bc_free = [0]
            st_bc_pe = [0]
            st_r_evac = [0]
            st_mask_dve = [0]
            st_maskT_pe = [0]
            st_maskT_evac = [0]

            xnf_free_t = [0, 0]
            xn_spill_t = [0, 0]

            def emit_xb_load(t):
                nx = t % 2
                p.wait("pool", a_pe, xnf_free_t[nx])
                p.wait("pool", xn_sp, xn_spill_t[nx])
                hts = []
                for half in range(2):
                    tt = dma("pool",
                             xnf[nx][:].rearrange("p (c t) -> p c t", c=DC)[:, half * 8:(half + 1) * 8, :],
                             x_fm.rearrange("(c p) t -> p c t", p=128)[:, half * 8:(half + 1) * 8,
                                                                      t * TILE:(t + 1) * TILE],
                             xb_ld[nx])
                    hts.append(tt)
                xb_half[t] = hts

            def emit_squares(t, grp):
                nx = t % 2
                half = 0 if grp < 2 else 1
                p.wait("act", xb_ld[nx], xb_half[t][half])
                prev = (t, grp - 2) if grp >= 2 else (t - 1, grp + 2)
                p.wait("act", a_pe, sq_pass_pe.get(prev, 0))
                for c4 in range(4):
                    c = grp * 4 + c4
                    slot = (grp % 2) * 4 + c4
                    p.act(lambda e, _nx=nx, _c=c, _s=slot: e.activation(
                        out=xsq[:, _s * TILE:(_s + 1) * TILE],
                        in_=xnf[_nx][:, _c * TILE:(_c + 1) * TILE], func=AF.Square))
                sq_mile[(t, grp)] = act_mile()

            def emit_sum_pass(t, c):
                nx = t % 2
                p.wait("te", xb_ld[nx], xb_half[t][0 if c < 8 else 1])
                if c == 0:
                    p.wait("te", a_dve, t_rows_free[t - 1])
                if c == DC - 1:
                    a_pe.tgt(1)
                    p.te(lambda e, _nx=nx, _c=c: e.matmul(
                        ps_sum[0:1, :], ones128_bf[:, 0:1],
                        xnf[_nx][:, _c * TILE:(_c + 1) * TILE],
                        start=(_c == 0), stop=True).then_inc(a_pe.h, 1))
                else:
                    p.te(lambda e, _nx=nx, _c=c: e.matmul(
                        ps_sum[0:1, :], ones128_bf[:, 0:1],
                        xnf[_nx][:, _c * TILE:(_c + 1) * TILE],
                        start=(_c == 0), stop=False))

            def emit_sq_pass(t, c):
                grp = c // 4
                slot = (grp % 2) * 4 + c % 4
                p.wait("te", a_act, sq_mile[(t, grp)])
                if c == 0:
                    p.wait("te", a_dve, t_rows_free[t - 1])
                if c % 4 == 3:
                    tp = a_pe.tgt(1)
                    p.te(lambda e, _s=slot, _c=c: e.matmul(
                        ps_sq[0:1, :], ones128_bf[:, 0:1],
                        xsq[:, _s * TILE:(_s + 1) * TILE],
                        start=(_c == 0), stop=(_c == DC - 1)).then_inc(a_pe.h, 1))
                    sq_pass_pe[(t, grp)] = tp
                    if c == DC - 1:
                        t_statdone[t] = tp
                else:
                    p.te(lambda e, _s=slot, _c=c: e.matmul(
                        ps_sq[0:1, :], ones128_bf[:, 0:1],
                        xsq[:, _s * TILE:(_s + 1) * TILE],
                        start=(_c == 0), stop=False))

            def emit_rows(t):
                p.wait("dve", a_pe, t_statdone[t])
                p.wait("dve", a_pe, st_bc_pe[0])
                dve_op(lambda e: e.tensor_scalar_mul(rows_bf[0:1, 0:TILE], ps_sum[0:1, :], 1.0 / D))
                dve_op(lambda e: e.tensor_scalar_mul(rows[0:1, 0:TILE], ps_sum[0:1, :], 1.0 / D))
                dve_fence()
                dve_op(lambda e: e.tensor_mul(rows[0:1, TILE:2 * TILE],
                                              rows[0:1, 0:TILE], rows[0:1, 0:TILE]))
                dve_fence()
                dve_op(lambda e: e.scalar_tensor_tensor(
                    out=rows[0:1, 0:TILE], in0=ps_sq[0:1, :], scalar=1.0 / D,
                    in1=rows[0:1, TILE:2 * TILE], op0=ALU.mult, op1=ALU.subtract))
                t_rows_free[t] = dve_mile()
                p.wait("act", a_dve, t_rows_free[t])
                p.act(lambda e: e.activation(out=rows[0:1, TILE:2 * TILE],
                                             in_=rows[0:1, 0:TILE],
                                             func=AF.Sqrt, bias=epsc[0:1, 0:1]))
                ta = act_mile()
                p.wait("dve", a_act, ta)
                dve_op(lambda e: e.reciprocal(rows[0:1, 2 * TILE:3 * TILE],
                                              rows[0:1, TILE:2 * TILE]))
                dve_fence()
                dve_op(lambda e: e.tensor_copy(out=rows_bf[0:1, TILE:2 * TILE],
                                               in_=rows[0:1, 2 * TILE:3 * TILE]))
                t_rstd_dve[t] = dve_mile()

            def emit_bcast(t):
                p.wait("te", a_dve, t_rows_free[t])
                p.wait("te", a_act, st_bc_free[0])
                tp = a_pe.tgt(1)
                p.te(lambda e: e.matmul(ps_bc[:, :], ones1_bf[0:1, :], rows_bf[0:1, 0:TILE],
                                        start=True, stop=True).then_inc(a_pe.h, 1))
                p.wait("act", a_pe, tp)
                p.wait("act", a_dve, t_norm_done.get(t - 1, 0))
                p.act(lambda e: e.copy(out=mrs[:, 0:TILE], in_=ps_bc[:, :]))
                tm = act_mile()
                p.wait("te", a_act, tm)
                p.wait("te", a_dve, t_rstd_dve[t])
                tp2 = a_pe.tgt(1)
                p.te(lambda e: e.matmul(ps_bc[:, :], ones1_bf[0:1, :],
                                        rows_bf[0:1, TILE:2 * TILE],
                                        start=True, stop=True).then_inc(a_pe.h, 1))
                st_bc_pe[0] = tp2
                p.wait("act", a_pe, tp2)
                p.act(lambda e: e.copy(out=mrs[:, TILE:2 * TILE], in_=ps_bc[:, :]))
                st_bc_free[0] = act_mile()
                t_mrs_ok[t] = st_bc_free[0]

            def emit_norm_quarter(t, qtr):
                nx = t % 2
                p.wait("dve", a_act, t_mrs_ok[t])
                p.wait("dve", a_pe, t_statdone[t])
                p.wait("dve", a_act, sq_mile[(t, 3)])
                for pr in range(2):
                    for c2 in range(2):
                        c = qtr * 4 + pr * 2 + c2
                        p.dve(lambda e, _nx=nx, _c=c, _c2=c2: e.tensor_tensor(
                            out=t1[:, _c2 * TILE:(_c2 + 1) * TILE],
                            in0=xnf[_nx][:, _c * TILE:(_c + 1) * TILE],
                            in1=mrs[:, 0:TILE], op=ALU.subtract))
                    dve_fence()
                    for c2 in range(2):
                        c = qtr * 4 + pr * 2 + c2
                        p.dve(lambda e, _nx=nx, _c=c, _c2=c2: e.tensor_tensor(
                            out=xnf[_nx][:, _c * TILE:(_c + 1) * TILE],
                            in0=t1[:, _c2 * TILE:(_c2 + 1) * TILE],
                            in1=mrs[:, TILE:2 * TILE], op=ALU.mult))
                if qtr == 3:
                    t_norm_done[t] = dve_mile()

            xn_spill_tile = {}

            def emit_spill(t):
                nx = t % 2
                p.wait("sp", a_dve, t_norm_done[t])
                for half in range(2):
                    xn_spill_t[nx] = dma(
                        "sp",
                        s_xn.rearrange("(c p) t -> p c t", p=128)[:, half * 8:(half + 1) * 8,
                                                                 t * TILE:(t + 1) * TILE],
                        xnf[nx][:].rearrange("p (c t) -> p c t", c=DC)[:, half * 8:(half + 1) * 8, :],
                        xn_sp)
                xn_spill_tile[t] = xn_spill_t[nx]

            def emit_router_sub(t, s):
                nx = t % 2
                sub = t * 4 + s
                p.wait("te", a_dve, t_norm_done[t])
                p.wait("te", g_ld, t_pro_pool)
                p.wait("te", a_act, st_r_evac[0])
                for dc in range(DC):
                    p.te(lambda e, _s=s, _dc=dc, _nx=nx: e.matmul(
                        ps_r[:], xnf[_nx][:, _dc * TILE + _s * 128: _dc * TILE + (_s + 1) * 128],
                        wrtb[:, _dc * NB:(_dc + 1) * NB], start=(_dc == 0), stop=False))
                tp_r = a_pe.tgt(1)
                p.te(lambda e: e.matmul(ps_r[:], ones1_bf[0:1, :], brr_bf[0:1, :],
                                        start=False, stop=True).then_inc(a_pe.h, 1))
                p.wait("act", a_pe, tp_r)
                p.wait("act", a_dve, st_mask_dve[0])
                p.act(lambda e: e.copy(out=r_sb[:], in_=ps_r[:]))
                ta_r = act_mile()
                st_r_evac[0] = ta_r
                p.wait("dve", a_act, ta_r)
                p.wait("dve", a_pe, st_maskT_pe[0])
                dve_op(lambda e: e.max(out=max8[:], in_=r_sb[:]))
                dve_fence()
                p.dve(lambda e: e.tensor_scalar(
                    out=mask_tm[:], in0=r_sb[:], scalar1=max8[:, 3:4], scalar2=None,
                    op0=ALU.is_ge))
                td_m = dve_mile()
                st_mask_dve[0] = td_m
                p.wait("te", a_dve, td_m)
                p.wait("te", a_act, st_maskT_evac[0])
                tp_m = a_pe.tgt(1)
                p.te(lambda e: e.transpose(ps_mT[0:NB, 0:128], mask_tm[:],
                                           idb[:]).then_inc(a_pe.h, 1))
                st_maskT_pe[0] = tp_m
                p.wait("act", a_pe, tp_m)
                p.act(lambda e, _sub=sub: e.copy(
                    out=mask_fm[:, _sub * 128:(_sub + 1) * 128], in_=ps_mT[0:NB, 0:128]))
                st_maskT_evac[0] = act_mile()

            qkv_free_t = [0, 0]
            vt_free_t = [0, 0]
            ps_qkv_free = [0, 0]
            ps_vt_free = [0]
            k0_st = [0]
            v0_st = [0]
            nrm_sq_free = [0, 0]
            nrm_row_evac = 0
            pending_norm = [None, None]

            def flush_norm(par2, wi):
                nonlocal nrm_row_evac
                if pending_norm[par2] is None:
                    return
                td_sq = pending_norm[par2]
                pending_norm[par2] = None
                p.wait("te", a_dve, td_sq)
                p.wait("te", a_act, nrm_row_evac)
                tp_n = a_pe.tgt(1)
                p.te(lambda e, _p2=par2: e.matmul(
                    ps_nrm[0:1, :], ones128_bf[:, 0:1], q2c[_p2][:],
                    start=True, stop=True).then_inc(a_pe.h, 1))
                nrm_sq_free[par2] = tp_n
                p.wait("act", a_pe, tp_n)
                p.act(lambda e: e.copy(out=nrm_row[0:1, :], in_=ps_nrm[0:1, :]))
                nrm_row_evac = act_mile()
                p.wait("dve", a_act, nrm_row_evac)
                dve_op(lambda e: e.reduce_max(nrm2[0:1, 0:1], nrm_row[0:1, :], axis=AX.X))
                dve_fence()
                dve_op(lambda e, _wi=wi: e.tensor_tensor(
                    out=qknorm[0:1, _wi:_wi + 1], in0=qknorm[0:1, _wi:_wi + 1],
                    in1=nrm2[0:1, 0:1], op=ALU.max))
                dve_fence()

            def emit_qkv(j, px, kind, wsl, hook=None):
                dst = {"q": s_q, "k": s_k, "v": None}[kind]
                bias = {"q": bqsc, "k": bksc, "v": bvs}[kind]
                scale = ALPHA if kind in ("q", "k") else 1.0
                wi = 0 if kind == "q" else 1
                for oc in range(DC):
                    par2 = oc % 2
                    if hook is not None:
                        hook(oc)
                    if kind in ("q", "k"):
                        flush_norm(par2, wi)
                    p.wait("te", a_act, ps_qkv_free[par2])
                    for dc in range(DC):
                        last = dc == DC - 1
                        if last:
                            a_pe.tgt(1)
                            p.te(lambda e, _oc=oc, _dc=dc, _px=px, _p2=par2, _w=wsl: e.matmul(
                                ps_qkv[_p2][:],
                                _w[:, _dc * D + _oc * 128: _dc * D + (_oc + 1) * 128],
                                xnf[_px][:, _dc * TILE:(_dc + 1) * TILE],
                                start=(_dc == 0), stop=True).then_inc(a_pe.h, 1))
                        else:
                            p.te(lambda e, _oc=oc, _dc=dc, _px=px, _p2=par2, _w=wsl: e.matmul(
                                ps_qkv[_p2][:],
                                _w[:, _dc * D + _oc * 128: _dc * D + (_oc + 1) * 128],
                                xnf[_px][:, _dc * TILE:(_dc + 1) * TILE],
                                start=(_dc == 0), stop=False))
                    t_mm = a_pe.n
                    p.wait("act", a_pe, t_mm)
                    p.wait("act", qk_st, qkv_free_t[par2])
                    p.act(lambda e, _p2=par2, _oc=oc, _b=bias, _sc=scale: e.activation(
                        out=qkv_st[_p2][:], in_=ps_qkv[_p2][:], func=AF.Identity,
                        bias=_b[:, _oc:_oc + 1], scale=_sc))
                    ta_e = act_mile()
                    ps_qkv_free[par2] = ta_e
                    if kind in ("q", "k"):
                        p.wait("sp", a_act, ta_e)
                        qkv_free_t[par2] = dma(
                            "sp", dst[oc * 128:(oc + 1) * 128, j * TILE:(j + 1) * TILE],
                            qkv_st[par2][:], qk_st)
                        if kind == "k" and oc == 0:
                            k0_st[0] = qkv_free_t[par2]
                        p.wait("dve", a_act, ta_e)
                        p.wait("dve", a_pe, nrm_sq_free[par2])
                        p.dve(lambda e, _p2=par2: e.tensor_tensor(
                            out=q2c[_p2][:], in0=qkv_st[_p2][:], in1=qkv_st[_p2][:],
                            op=ALU.mult))
                        pending_norm[par2] = dve_mile()
                    else:
                        p.wait("te", a_act, ta_e)
                        p.wait("te", a_act, ps_vt_free[0])
                        for s4 in range(4):
                            last = s4 == 3
                            if last:
                                a_pe.tgt(1)
                                p.te(lambda e, _p2=par2, _s4=s4: e.transpose(
                                    ps_mT[:, _s4 * 128:(_s4 + 1) * 128],
                                    qkv_st[_p2][:, _s4 * 128:(_s4 + 1) * 128],
                                    idb[:]).then_inc(a_pe.h, 1))
                            else:
                                p.te(lambda e, _p2=par2, _s4=s4: e.transpose(
                                    ps_mT[:, _s4 * 128:(_s4 + 1) * 128],
                                    qkv_st[_p2][:, _s4 * 128:(_s4 + 1) * 128],
                                    idb[:]))
                        t_vt = a_pe.n
                        p.wait("act", a_pe, t_vt)
                        p.wait("act", qk_st, vt_free_t[par2])
                        p.act(lambda e, _p2=par2: e.copy(
                            out=vt_st[_p2][:], in_=ps_mT[:, :]))
                        ta_v = act_mile()
                        ps_vt_free[0] = ta_v
                        p.wait("sp", a_act, ta_v)
                        vt_free_t[par2] = dma(
                            "sp",
                            s_v[oc][:, j * 4:(j + 1) * 4, :],
                            vt_st[par2][:].rearrange("p (c f) -> p c f", c=4),
                            qk_st)
                        if oc == 0:
                            v0_st[0] = vt_free_t[par2]
                        qkv_free_t[par2] = 0

            p.wait("te", g_ld, t_idb)
            p.wait("te", g_dve, t_bias)
            p.wait("act", g_dve, t_bias)
            emit_xb_load(0)
            emit_squares(0, 0)
            emit_squares(0, 1)
            for c in range(DC):
                emit_sum_pass(0, c)
            for c in range(8):
                emit_sq_pass(0, c)
            emit_squares(0, 2)
            emit_squares(0, 3)
            for c in range(8, DC):
                emit_sq_pass(0, c)
            emit_rows(0)
            emit_bcast(0)
            for qtr in range(4):
                emit_norm_quarter(0, qtr)
            emit_spill(0)
            for s in range(4):
                emit_router_sub(0, s)

            for dc in range(DC):
                dma("pool", wslotA[:, dc * D:(dc + 1) * D], wq[dc * 128:(dc + 1) * 128, :], w_ld)
            t_wq = w_ld.n
            t_wk_holder = [0]

            def qhook_factory(j):
                t = j + 1
                base_map = {5: 0, 6: 2, 8: 4, 10: 6, 12: 8, 13: 10, 14: 12, 15: 14}

                def hook(oc):
                    if j == 0 and oc == 1:
                        for dc in range(DC):
                            dma("pool", wslotB[:, dc * D:(dc + 1) * D],
                                wk[dc * 128:(dc + 1) * 128, :], w_ld)
                        t_wk_holder[0] = w_ld.n
                    if t >= NTILE:
                        return
                    if oc == 0:
                        emit_xb_load(t)
                    elif oc == 3:
                        emit_squares(t, 0)
                    elif oc == 4:
                        emit_squares(t, 1)
                    elif oc == 7:
                        emit_squares(t, 2)
                    elif oc == 11:
                        emit_squares(t, 3)
                    elif oc in base_map:
                        c = base_map[oc]
                        emit_sum_pass(t, c)
                        emit_sum_pass(t, c + 1)
                        emit_sq_pass(t, c)
                        emit_sq_pass(t, c + 1)
                return hook

            def khook_factory(j):
                t = j + 1

                def hook(oc):
                    if t >= NTILE:
                        return
                    if oc == 0:
                        emit_rows(t)
                    elif oc == 1:
                        emit_bcast(t)
                    elif oc in (3, 4, 5, 6):
                        emit_norm_quarter(t, oc - 3)
                    elif oc == 7:
                        emit_spill(t)
                    elif oc in (9, 11, 13, 15):
                        emit_router_sub(t, (oc - 9) // 2)
                return hook

            t_q_done = 0
            for j in range(NTILE):
                px = j % 2
                p.wait("te", a_dve, t_norm_done[j])
                p.wait("te", w_ld, t_wq)
                emit_qkv(j, px, "q", wslotA, qhook_factory(j))
                if j == NTILE - 1:
                    t_q_done = a_pe.n
                p.wait("te", w_ld, t_wk_holder[0])
                emit_qkv(j, px, "k", wslotB, khook_factory(j))
                xnf_free_t[px] = a_pe.n
            flush_norm(0, 0)
            flush_norm(1, 1)

            p.dve(lambda e: e.tensor_scalar(
                out=negrow[0:1, :], in0=qknorm[0:1, 0:1], scalar1=qknorm[0:1, 1:2],
                scalar2=-0.5, op0=ALU.add, op1=ALU.mult))
            td_nb = dve_mile()
            p.wait("te", a_dve, td_nb)
            tp_nb = a_pe.tgt(1)
            p.te(lambda e: e.matmul(ps_r[:, 0:1], ones1_f[0:1, :], negrow[0:1, :],
                                    start=True, stop=True).then_inc(a_pe.h, 1))
            p.wait("act", a_pe, tp_nb)
            p.act(lambda e: e.copy(out=negb[:], in_=ps_r[:, 0:1]))
            t_negb_a = act_mile()
            p.act(lambda e: e.activation(out=negrow[0:1, :], in_=negrow[0:1, :],
                                         func=AF.Exp, scale=0.0))

            p.wait("pool", a_pe, t_q_done)
            for dc in range(DC):
                dma("pool", wslotA[:, dc * D:(dc + 1) * D], wv[dc * 128:(dc + 1) * 128, :], w_ld)
            t_wv = w_ld.n
            p.wait("te", w_ld, t_wv)
            ps_vt_free[0] = st_maskT_evac[0]
            for j in range(NTILE):
                px = j % 2
                p.wait("pool", a_pe, xnf_free_t[px])
                p.wait("pool", xn_sp, xn_spill_tile[j])
                t_r = 0
                for half in range(2):
                    t_r = dma("pool", xnf[px][:].rearrange("p (c t) -> p c t", c=DC)[:, half * 8:(half + 1) * 8, :],
                              s_xn.rearrange("(c p) t -> p c t", p=128)[:, half * 8:(half + 1) * 8,
                                                                       j * TILE:(j + 1) * TILE],
                              xn_rld[px])
                p.wait("te", xn_rld[px], t_r)
                emit_qkv(j, px, "v", wslotA)
                xnf_free_t[px] = a_pe.n

            p.wait("sp", qk_st, qk_st.n)
            p.wait("sp", xn_sp, xn_sp.n)
            p.sp(lambda e: e.nop().then_inc(phase.h, 1))
            p.te(lambda e: e.nop().then_inc(phase.h, 1))
            p.act(lambda e: e.nop().then_inc(phase.h, 1))
            p.dve(lambda e: e.nop().then_inc(phase.h, 1))
            p.pool(lambda e: e.nop().then_inc(phase.h, 1))
            phase.n += 5
            t_phase_a = phase.n

        b_pe = mksem("b_pe")
        b_act = mksem("b_act")
        b_dve = mksem("b_dve")
        wo_ld = mksem("wo_ld")
        q_ld = [mksem("q_ld0"), mksem("q_ld1"), mksem("q_ld2"), mksem("q_ld3")]
        at_st = mksem("at_st")
        at_ld = [mksem("at_ld0"), mksem("at_ld1")]
        ao_st = mksem("ao_st")

        with ExitStack() as bctx:
            bsb = lambda n, s, d: bctx.enter_context(nc.sbuf_tensor(n, s, d))
            bps = lambda n, s, d: bctx.enter_context(nc.psum_tensor(n, s, d))
            ksb = [bsb("ksb0", [128, T], BF16), bsb("ksb1", [128, T], BF16)]
            vsb = [bsb("vsb0", [128, T], BF16), bsb("vsb1", [128, T], BF16)]
            qsb = [bsb("qsb0", [128, TILE], BF16), bsb("qsb1", [128, TILE], BF16),
                   bsb("qsb2", [128, TILE], BF16), bsb("qsb3", [128, TILE], BF16)]
            o_tmp = [bsb("o_tmp0", [128, TILE], F32), bsb("o_tmp1", [128, TILE], F32)]
            rzb_sb = [bsb("rzb0", [128, TILE], F32), bsb("rzb1", [128, TILE], F32)]
            ones_sq = bsb("ones_sq", [128, 128], BF16)
            cbT = bsb("cbT", [128, 4 * TILE], BF16)
            pTs = bsb("pTs", [128, 4 * T], BF16)
            oout = [bsb("oout0", [128, TILE], BF16), bsb("oout1", [128, TILE], BF16)]
            woB = bsb("woB", [128, DC * D], BF16)
            atb = [bsb("atb0", [128, DC * TILE], BF16), bsb("atb1", [128, DC * TILE], BF16)]
            aost = [bsb("aost0", [128, TILE], F32), bsb("aost1", [128, TILE], F32)]
            jdve = bsb("jdve", [128, 1], F32)
            jact = bsb("jact", [128, 1], F32)

            ps_sT = [bps("ps_sT0", [128, 1024], F32), bps("ps_sT1", [128, 1024], F32)]
            ps_o = [bps("ps_o0", [128, TILE], F32), bps("ps_o1", [128, TILE], F32)]
            ps_z = [bps("ps_z0", [128, TILE], F32), bps("ps_z1", [128, TILE], F32)]

            def bdve_mile():
                t = b_dve.tgt(1)
                p.dve(lambda e: e.tensor_copy(out=jdve[:], in_=epsc[:]).then_inc(b_dve.h, 1))
                return t

            def bdve_fence():
                t = bdve_mile()
                p.wait("dve", b_dve, t)

            def bact_mile():
                t = b_act.tgt(1)
                p.act(lambda e: e.copy(out=jact[:], in_=epsc[:]).then_inc(b_act.h, 1))
                return t

            for key in Prog.KEYS:
                p.wait(key, phase, t_phase_a)
            p.dve(lambda e: e.memset(ones_sq[:], 1.0))

            t_cb = dma("pool", cbT[:], causal[:, :], wo_ld)
            p.wait("dve", wo_ld, t_cb)
            cast_pieces = []
            for i4 in range(16):
                cast_pieces.append(lambda _i=i4: dma(
                    "pool", s_w1[_i * 128:(_i + 1) * 128, :], w1t[_i * 128:(_i + 1) * 128, :],
                    s_w1_ld))
            for i4 in range(16):
                cast_pieces.append(lambda _i=i4: dma("pool", s_w2[_i], w2p[_i], s_w2_ld))

            kv_free_pe = [0, 0]
            q_free_pe = [0, 0, 0, 0]
            oout_free = [0, 0]
            ps_sT_free = [0, 0]
            ps_o_free = [0, 0]
            z_free = [0, 0]
            o_used = [0, 0]

            p.wait("act", a_act, t_negb_a)

            t_kh = [0] * NH
            t_vh = [0] * NH

            def load_kv(h):
                hp_ = h % 2
                p.wait("pool", b_pe, kv_free_pe[hp_])
                t_kh[h] = dma("pool", ksb[hp_][:], s_k[h * 128:(h + 1) * 128, :], k_ld[hp_])
                t_vh[h] = dma("pool", vsb[hp_][:].rearrange("p (c f) -> p c f", c=T // 128),
                              s_v[h], v_ld[hp_])

            load_kv(0)
            for dc in range(DC):
                dma("pool", woB[:, dc * D:(dc + 1) * D], wo[dc * 128:(dc + 1) * 128, :], wo_ld)
            t_wo = wo_ld.n
            for h in range(NH):
                hp = h % 2
                if h + 1 < NH:
                    load_kv(h + 1)
                t_k, t_v = t_kh[h], t_vh[h]

                for i in range(NTILE):
                    qp = (h * NTILE + i) % 4
                    p.wait("sp", b_pe, q_free_pe[qp])
                    t_q = dma("sp", qsb[qp][:],
                              s_q[h * 128:(h + 1) * 128, i * TILE:(i + 1) * TILE], q_ld[qp])
                    if cast_pieces:
                        cast_pieces.pop(0)()
                    p.wait("te", q_ld[qp], t_q)

                    bp = (h * NTILE + i) % 2
                    nkc = 4 * (i + 1)
                    npair = nkc // 2
                    pend = []

                    def emit_zo(pd):
                        pr, ws_, wt_ = pd
                        p.wait("te", ws_, wt_)
                        if pr == 0:
                            p.wait("te", b_act, ps_o_free[bp])
                            p.wait("te", b_dve, z_free[bp])
                            p.wait("te", v_ld[hp], t_v)
                        for kidx, kc in enumerate((2 * pr, 2 * pr + 1)):
                            st_ = pr == 0 and kidx == 0
                            sp_ = pr == npair - 1 and kidx == 1
                            p.te(lambda e, _bp=bp, _kc=kc, _st=st_, _sp=sp_: e.matmul(
                                ps_z[_bp][:, :], ones_sq[:, :],
                                pTs[:, _kc * 512:(_kc + 1) * 512],
                                start=_st, stop=_sp))
                            if sp_:
                                b_pe.tgt(1)
                                p.te(lambda e, _bp=bp, _hp=hp, _kc=kc, _st=st_, _sp=sp_: e.matmul(
                                    ps_o[_bp][:], vsb[_hp][:, _kc * 128:(_kc + 1) * 128],
                                    pTs[:, _kc * 512:(_kc + 1) * 512],
                                    start=_st, stop=_sp).then_inc(b_pe.h, 1))
                            else:
                                p.te(lambda e, _bp=bp, _hp=hp, _kc=kc, _st=st_, _sp=sp_: e.matmul(
                                    ps_o[_bp][:], vsb[_hp][:, _kc * 128:(_kc + 1) * 128],
                                    pTs[:, _kc * 512:(_kc + 1) * 512],
                                    start=_st, stop=_sp))

                    for pair in range(npair):
                        slot = pair % 2
                        kc0, kc1 = 2 * pair, 2 * pair + 1
                        p.wait("te", b_act, ps_sT_free[slot])
                        p.wait("te", k_ld[hp], t_k)
                        for kidx, kc in enumerate((kc0, kc1)):
                            last = kidx == 1
                            if last:
                                tp_s = b_pe.tgt(1)
                                p.te(lambda e, _hp=hp, _qp=qp, _kc=kc, _sl=slot, _ki=kidx: e.matmul(
                                    ps_sT[_sl][:, _ki * 512:(_ki + 1) * 512],
                                    ksb[_hp][:, _kc * 128:(_kc + 1) * 128],
                                    qsb[_qp][:], start=True, stop=True).then_inc(b_pe.h, 1))
                            else:
                                p.te(lambda e, _hp=hp, _qp=qp, _kc=kc, _sl=slot, _ki=kidx: e.matmul(
                                    ps_sT[_sl][:, _ki * 512:(_ki + 1) * 512],
                                    ksb[_hp][:, _kc * 128:(_kc + 1) * 128],
                                    qsb[_qp][:], start=True, stop=True))
                        p.wait("act", b_pe, tp_s)
                        if pair >= 2 * i:
                            p.act(lambda e, _sl=slot, _pr=pair: e.activation(
                                out=pTs[:, _pr * 1024:(_pr + 1) * 1024],
                                in_=ps_sT[_sl][:], func=AF.Exp, bias=negb[:, 0:1]))
                            ps_sT_free[slot] = bact_mile()
                        else:
                            ps_sT_free[slot] = b_act.tgt(1)
                            p.act(lambda e, _sl=slot, _pr=pair: e.activation(
                                out=pTs[:, _pr * 1024:(_pr + 1) * 1024],
                                in_=ps_sT[_sl][:], func=AF.Exp,
                                bias=negb[:, 0:1]).then_inc(b_act.h, 1))
                        if pair >= 2 * i:
                            s0 = (pair - 2 * i) * 2
                            p.wait("dve", b_act, ps_sT_free[slot])
                            p.dve(lambda e, _pr=pair, _s0=s0: e.tensor_tensor(
                                out=pTs[:, _pr * 1024:(_pr + 1) * 1024],
                                in0=pTs[:, _pr * 1024:(_pr + 1) * 1024],
                                in1=cbT[:, _s0 * 512:(_s0 + 2) * 512], op=ALU.mult))
                            pend.append((pair, b_dve, bdve_mile()))
                        else:
                            pend.append((pair, b_act, ps_sT_free[slot]))
                        if len(pend) > 2:
                            emit_zo(pend.pop(0))
                    while pend:
                        emit_zo(pend.pop(0))
                    t_scores_done = b_pe.n

                    op_ = (h * NTILE + i) % 2
                    p.wait("act", b_pe, t_scores_done)
                    p.wait("act", b_dve, o_used[bp])
                    p.act(lambda e, _bp=bp: e.copy(out=o_tmp[_bp][:], in_=ps_o[_bp][:]))
                    ps_o_free[bp] = bact_mile()
                    p.wait("dve", b_pe, t_scores_done)
                    p.dve(lambda e, _bp=bp: e.reciprocal(rzb_sb[_bp][:], ps_z[_bp][:, :]))
                    z_free[bp] = bdve_mile()
                    p.wait("dve", b_dve, z_free[bp])
                    p.wait("dve", b_act, ps_o_free[bp])
                    p.wait("dve", at_st, oout_free[op_])
                    p.dve(lambda e, _bp=bp, _op=op_: e.tensor_tensor(
                        out=oout[_op][:], in0=o_tmp[_bp][:], in1=rzb_sb[_bp][:], op=ALU.mult))
                    o_used[bp] = bdve_mile()
                    p.wait("pool", b_dve, o_used[bp])
                    oout_free[op_] = dma(
                        "pool", s_attn[h * 128:(h + 1) * 128, i * TILE:(i + 1) * TILE],
                        oout[op_][:], at_st)
                    if h == NH - 1 and i == 0:
                        p.wait("pool", at_st, at_st.n)
                        for half in range(2):
                            t_at_pre = dma(
                                "pool",
                                atb[0][:].rearrange("p (c t) -> p c t", c=DC)[:, half * 8:(half + 1) * 8, :],
                                s_attn.rearrange("(c p) t -> p c t", p=128)[:, half * 8:(half + 1) * 8,
                                                                           0:TILE],
                                at_ld[0])
                    q_free_pe[qp] = t_scores_done
                kv_free_pe[hp] = b_pe.n

            t_w1cast = s_w1_ld.n
            t_w2cast = s_w2_ld.n
            t_b1_st = at_st.n

            def load_atb(j, gate):
                ap_ = j % 2
                p.wait("pool", at_st, gate)
                p.wait("pool", b_pe, atb_free[ap_])
                t_at = 0
                for half in range(2):
                    t_at = dma("pool",
                               atb[ap_][:].rearrange("p (c t) -> p c t", c=DC)[:, half * 8:(half + 1) * 8, :],
                               s_attn.rearrange("(c p) t -> p c t", p=128)[:, half * 8:(half + 1) * 8,
                                                                          j * TILE:(j + 1) * TILE],
                               at_ld[ap_])
                return t_at

            atb_free = [0, 0]
            aost_free = [0, 0]
            ps_b2_free = list(ps_sT_free)
            p.wait("sp", at_st, t_b1_st)
            p.wait("te", wo_ld, t_wo)
            for j in range(NTILE):
                ap_ = j % 2
                t_at = t_at_pre if j == 0 else load_atb(j, t_b1_st)
                p.wait("te", at_ld[ap_], t_at)
                for dm in range(DC):
                    par = dm % 2
                    p.wait("te", b_act, ps_b2_free[par])
                    for hfc in range(DC):
                        last = hfc == DC - 1
                        if last:
                            tp_b2 = b_pe.tgt(1)
                            p.te(lambda e, _a=ap_, _dm=dm, _h=hfc, _p=par: e.matmul(
                                ps_sT[_p][:, 0:512],
                                woB[:, _h * D + _dm * 128: _h * D + (_dm + 1) * 128],
                                atb[_a][:, _h * TILE:(_h + 1) * TILE],
                                start=(_h == 0), stop=True).then_inc(b_pe.h, 1))
                        else:
                            p.te(lambda e, _a=ap_, _dm=dm, _h=hfc, _p=par: e.matmul(
                                ps_sT[_p][:, 0:512],
                                woB[:, _h * D + _dm * 128: _h * D + (_dm + 1) * 128],
                                atb[_a][:, _h * TILE:(_h + 1) * TILE],
                                start=(_h == 0), stop=False))
                    p.wait("act", b_pe, tp_b2)
                    p.wait("act", ao_st, aost_free[par])
                    p.act(lambda e, _p=par, _dm=dm: e.activation(
                        out=aost[_p][:], in_=ps_sT[_p][:, 0:512], func=AF.Identity,
                        bias=bos[:, _dm:_dm + 1]))
                    ps_b2_free[par] = bact_mile()
                    p.wait("sp", b_act, ps_b2_free[par])
                    aost_free[par] = dma(
                        "sp", s_ao[dm * 128:(dm + 1) * 128, j * TILE:(j + 1) * TILE],
                        aost[par][:], ao_st)
                atb_free[ap_] = b_pe.n

            p.wait("sp", ao_st, ao_st.n)
            p.sp(lambda e: e.nop().then_inc(phase.h, 1))
            p.te(lambda e: e.nop().then_inc(phase.h, 1))
            p.act(lambda e: e.nop().then_inc(phase.h, 1))
            p.dve(lambda e: e.nop().then_inc(phase.h, 1))
            p.pool(lambda e: e.nop().then_inc(phase.h, 1))
            phase.n += 5
            t_phase_b = phase.n

        c_pe = mksem("c_pe")
        c_act = mksem("c_act")
        c_dve = mksem("c_dve")
        xnb_ld = [mksem("xnb_ld0"), mksem("xnb_ld1")]
        w1_ld = [mksem("w1_ld0"), mksem("w1_ld1")]
        w2_ld = [mksem("w2_ld0"), mksem("w2_ld1")]
        xs_ld = [mksem("xs_ld0"), mksem("xs_ld1")]
        y_st = mksem("y_st")

        with ExitStack() as cctx:
            csb = lambda n, s, d: cctx.enter_context(nc.sbuf_tensor(n, s, d))
            cps = lambda n, s, d: cctx.enter_context(nc.psum_tensor(n, s, d))
            xnc = [csb("xnc0", [128, DC * TILE], BF16), csb("xnc1", [128, DC * TILE], BF16)]
            h_sb = csb("h_sb", [128, FC * TILE], BF16)
            w1g = [csb("w1g0", [128, DC * TILE], BF16), csb("w1g1", [128, DC * TILE], BF16)]
            w2c = [csb("w2c0", [128, FC * 128], BF16), csb("w2c1", [128, FC * 128], BF16)]
            mbb = [csb("mbb0", [128, TILE], BF16), csb("mbb1", [128, TILE], BF16)]
            gel = [csb("gel0", [128, TILE], BF16), csb("gel1", [128, TILE], BF16)]
            xsl = [csb("xsl0", [128, TILE], F32), csb("xsl1", [128, TILE], F32)]
            aosl = [csb("aosl0", [128, TILE], F32), csb("aosl1", [128, TILE], F32)]
            accb = [csb("accb0", [128, TILE], F32), csb("accb1", [128, TILE], F32)]
            yst_b = [csb("yst0", [128, TILE], F32), csb("yst1", [128, TILE], F32)]
            jd2 = csb("jd2", [128, 1], F32)
            ja2 = csb("ja2", [128, 1], F32)

            ps_h = [cps("ps_h0", [128, TILE], F32), cps("ps_h1", [128, TILE], F32)]
            ps_mb = cps("ps_mb", [128, TILE], F32)
            ps_y = [cps("ps_y0", [128, TILE], F32), cps("ps_y1", [128, TILE], F32)]

            def cdve_mile():
                t = c_dve.tgt(1)
                p.dve(lambda e: e.tensor_copy(out=jd2[:], in_=epsc[:]).then_inc(c_dve.h, 1))
                return t

            def cact_mile():
                t = c_act.tgt(1)
                p.act(lambda e: e.copy(out=ja2[:], in_=epsc[:]).then_inc(c_act.h, 1))
                return t

            for key in Prog.KEYS:
                p.wait(key, phase, t_phase_b)
            p.wait("sp", s_w1_ld, t_w1cast)
            p.wait("sp", s_w2_ld, t_w2cast)

            xnc_free = [0, 0]
            w1_free = [0, 0]
            w2_free = [0, 0]
            mbb_free = [0, 0]
            gel_free = [0, 0]
            ps_h_free = [0, 0]
            ps_mb_free = 0
            ps_y_free = [0, 0]
            xs_free = [0, 0]
            yst_free = [0, 0]
            h_free_pe = 0

            for j in range(NTILE):
                jp = j % 2
                p.wait("sp", c_pe, xnc_free[jp])
                t_xn = 0
                for half in range(2):
                    t_xn = dma("sp",
                               xnc[jp][:].rearrange("p (c t) -> p c t", c=DC)[:, half * 8:(half + 1) * 8, :],
                               s_xn.rearrange("(c p) t -> p c t", p=128)[:, half * 8:(half + 1) * 8,
                                                                        j * TILE:(j + 1) * TILE],
                               xnb_ld[jp])
                for g in range(NB):
                    wp = g % 2
                    p.wait("sp", c_pe, w1_free[wp])
                    t_w1 = 0
                    for half in range(2):
                        t_w1 = dma("sp",
                                   w1g[wp][:].rearrange("p (c f) -> p c f", c=DC)[:, half * 8:(half + 1) * 8, :],
                                   s_w1.rearrange("(c p) f -> p c f", p=128)[:, half * 8:(half + 1) * 8,
                                                                            g * TILE:(g + 1) * TILE],
                                   w1_ld[wp])
                    p.wait("te", c_act, ps_mb_free)
                    p.wait("te", c_dve, mbb_free[wp])
                    tp_mb = c_pe.tgt(1)
                    p.te(lambda e, _g=g, _j=j: e.matmul(
                        ps_mb[:], selb[0:16, _g * 128:(_g + 1) * 128],
                        mask_fm[0:16, _j * TILE:(_j + 1) * TILE],
                        start=True, stop=True).then_inc(c_pe.h, 1))
                    p.wait("act", c_pe, tp_mb)
                    p.act(lambda e, _wp=wp: e.copy(out=mbb[_wp][:], in_=ps_mb[:]))
                    ps_mb_free = cact_mile()
                    t_mb = ps_mb_free
                    for fc4 in range(4):
                        fc = 4 * g + fc4
                        hp2 = fc % 2
                        p.wait("te", w1_ld[wp], t_w1)
                        p.wait("te", xnb_ld[jp], t_xn)
                        p.wait("te", c_act, ps_h_free[hp2])
                        for dc in range(DC):
                            last = dc == DC - 1
                            if last:
                                tp_h = c_pe.tgt(1)
                                p.te(lambda e, _wp=wp, _f=fc4, _dc=dc, _jp=jp, _h2=hp2: e.matmul(
                                    ps_h[_h2][:],
                                    w1g[_wp][:, _dc * TILE + _f * 128: _dc * TILE + (_f + 1) * 128],
                                    xnc[_jp][:, _dc * TILE:(_dc + 1) * TILE],
                                    start=(_dc == 0), stop=True).then_inc(c_pe.h, 1))
                            else:
                                p.te(lambda e, _wp=wp, _f=fc4, _dc=dc, _jp=jp, _h2=hp2: e.matmul(
                                    ps_h[_h2][:],
                                    w1g[_wp][:, _dc * TILE + _f * 128: _dc * TILE + (_f + 1) * 128],
                                    xnc[_jp][:, _dc * TILE:(_dc + 1) * TILE],
                                    start=(_dc == 0), stop=False))
                        p.wait("act", c_pe, tp_h)
                        p.wait("act", c_dve, gel_free[hp2])
                        p.act(lambda e, _h2=hp2: e.activation(
                            out=gel[_h2][:], in_=ps_h[_h2][:], func=AF.Gelu))
                        ps_h_free[hp2] = cact_mile()
                        p.wait("dve", c_act, ps_h_free[hp2])
                        p.wait("dve", c_act, t_mb)
                        p.wait("dve", c_pe, h_free_pe)
                        p.dve(lambda e, _h2=hp2, _wp=wp, _fc=fc: e.tensor_tensor(
                            out=h_sb[:, _fc * TILE:(_fc + 1) * TILE],
                            in0=gel[_h2][:], in1=mbb[_wp][:], op=ALU.mult))
                        tmil = cdve_mile()
                        gel_free[hp2] = tmil
                        if fc4 == 3:
                            mbb_free[wp] = tmil
                    w1_free[wp] = c_pe.n
                t_h_done = c_dve.n

                for dm in range(DC):
                    wp2 = dm % 2
                    p.wait("sp", c_pe, w2_free[wp2])
                    t_w2 = dma("sp", w2c[wp2][:], s_w2[dm], w2_ld[wp2])
                    p.wait("pool", c_dve, xs_free[wp2])
                    dma("pool", xsl[wp2][:],
                        x_fm[dm * 128:(dm + 1) * 128, j * TILE:(j + 1) * TILE], xs_ld[wp2])
                    t_xs = dma("pool", aosl[wp2][:],
                               s_ao[dm * 128:(dm + 1) * 128, j * TILE:(j + 1) * TILE], xs_ld[wp2])
                    p.wait("te", w2_ld[wp2], t_w2)
                    p.wait("te", c_dve, t_h_done)
                    p.wait("te", c_dve, ps_y_free[wp2])
                    for fc in range(FC):
                        last = fc == FC - 1
                        if last:
                            tp_y = c_pe.tgt(1)
                            p.te(lambda e, _w=wp2, _fc=fc: e.matmul(
                                ps_y[_w][:], w2c[_w][:, _fc * 128:(_fc + 1) * 128],
                                h_sb[:, _fc * TILE:(_fc + 1) * TILE],
                                start=(_fc == 0), stop=True).then_inc(c_pe.h, 1))
                        else:
                            p.te(lambda e, _w=wp2, _fc=fc: e.matmul(
                                ps_y[_w][:], w2c[_w][:, _fc * 128:(_fc + 1) * 128],
                                h_sb[:, _fc * TILE:(_fc + 1) * TILE],
                                start=(_fc == 0), stop=False))
                    w2_free[wp2] = c_pe.n
                    p.wait("dve", xs_ld[wp2], t_xs)
                    p.dve(lambda e, _w=wp2: e.tensor_tensor(
                        out=accb[_w][:], in0=xsl[_w][:], in1=aosl[_w][:], op=ALU.add))
                    xs_free[wp2] = cdve_mile()
                    p.wait("dve", c_dve, xs_free[wp2])
                    p.wait("dve", c_pe, tp_y)
                    p.wait("dve", y_st, yst_free[wp2])
                    p.dve(lambda e, _w=wp2: e.tensor_tensor(
                        out=yst_b[_w][:], in0=ps_y[_w][:], in1=accb[_w][:], op=ALU.add))
                    ps_y_free[wp2] = cdve_mile()
                    p.wait("pool", c_dve, ps_y_free[wp2])
                    yst_free[wp2] = dma(
                        "pool", y[dm * 128:(dm + 1) * 128, j * TILE:(j + 1) * TILE],
                        yst_b[wp2][:], y_st)
                h_free_pe = c_pe.n
                xnc_free[jp] = c_pe.n

            p.wait("pool", y_st, y_st.n)
            p.wait("sp", y_st, y_st.n)

        with nc.Block() as blk:
            @blk.tensor
            def _(e):
                for f in p.q["te"]:
                    f(e)

            @blk.scalar
            def _(e):
                for f in p.q["act"]:
                    f(e)

            @blk.vector
            def _(e):
                for f in p.q["dve"]:
                    f(e)

            @blk.gpsimd
            def _(e):
                for f in p.q["pool"]:
                    f(e)

            @blk.sync
            def _(e):
                for f in p.q["sp"]:
                    f(e)

    return nc


def _host_prep(inputs):
    x = np.ascontiguousarray(inputs["L0_attn_query"])
    shared = {
        "wq": np.ascontiguousarray(inputs["L0_attn_w_q"]),
        "wk": np.ascontiguousarray(inputs["L0_attn_w_k"]),
        "wv": np.ascontiguousarray(inputs["L0_attn_w_v"]),
        "wo": np.ascontiguousarray(inputs["L0_attn_w_o"]),
        "w1t": np.ascontiguousarray(inputs["L0_ffn_W1"].T),
        "w2p": np.ascontiguousarray(
            inputs["L0_ffn_W2"].reshape(DC, 128, FC, 128).transpose(0, 3, 2, 1)
            .reshape(DC, 128, FC * 128)),
        "wrt": np.ascontiguousarray(inputs["L0_ffn_Wr"].T),
        "bq2": np.ascontiguousarray(inputs["L0_attn_b_q"].reshape(16, 128).T),
        "bk2": np.ascontiguousarray(inputs["L0_attn_b_k"].reshape(16, 128).T),
        "bv2": np.ascontiguousarray(inputs["L0_attn_b_v"].reshape(16, 128).T),
        "bo2": np.ascontiguousarray(inputs["L0_attn_b_o"].reshape(16, 128).T),
        "brr": np.ascontiguousarray(inputs["L0_ffn_br"].reshape(1, NB)),
    }
    cbias = np.zeros((128, 4, TILE), dtype=np.float32)
    pp = np.arange(128)[:, None]
    ff = np.arange(TILE)[None, :]
    for s in range(4):
        cbias[:, s, :] = (ff >= s * 128 + pp).astype(np.float32)
    shared["causal"] = np.ascontiguousarray(cbias.reshape(128, 4 * TILE))
    shared["identf"] = np.eye(128, dtype=np.float32)
    selm = np.zeros((NB, NB * 128), dtype=np.float32)
    for e_ in range(NB):
        selm[e_, e_ * 128:(e_ + 1) * 128] = 1.0
    shared["self_"] = selm
    per_core = []
    for c in range(8):
        m = dict(shared)
        m["x_fm"] = np.ascontiguousarray(x[c].T)
        per_core.append(m)
    return per_core


def kernel(**inputs):
    per_core = _host_prep(inputs)
    nc = build(debug=False)
    res = run_bass_kernel_spmd(nc, per_core, core_ids=list(range(8)))
    out = np.stack([np.ascontiguousarray(r["y"].T) for r in res.results], axis=0)
    return out


# version 18 (final)
# speedup vs baseline: 1.2652x; 1.2085x over previous
import numpy as np
from contextlib import ExitStack

import concourse.bass as bass
import concourse.mybir as mybir
from concourse.bass_utils import run_bass_kernel_spmd

F32 = mybir.dt.float32
BF16 = mybir.dt.bfloat16
AF = mybir.ActivationFunctionType
ALU = mybir.AluOpType
AX = mybir.AxisListType

T = 4096
D = 2048
NH = 16
DH = 128
FF = 8192
NB = 16
BS = 512
TILE = 512
NTILE = T // TILE
DC = D // 128
FC = FF // 128
ALPHA = float(128.0 ** -0.25)
EPS = 1e-5


class Sem:

    def __init__(self, handle):
        self.h = handle
        self.n = 0

    def tgt(self, k):
        self.n += k
        return self.n


class Prog:

    KEYS = ("te", "act", "dve", "pool", "sp")

    def __init__(self):
        self.q = {k: [] for k in self.KEYS}
        self.hw = {}

    def te(self, f):
        self.q["te"].append(f)

    def act(self, f):
        self.q["act"].append(f)

    def dve(self, f):
        self.q["dve"].append(f)

    def pool(self, f):
        self.q["pool"].append(f)

    def sp(self, f):
        self.q["sp"].append(f)

    def wait(self, key, sem, n):
        if n <= 0 or self.hw.get((key, sem.h.num), -1) >= n:
            return
        self.hw[(key, sem.h.num)] = n
        self.q[key].append(lambda e, _h=sem.h, _n=n: e.wait_ge(_h, _n))


def build(debug=False):
    nc = bass.Bass("TRN2", target_bir_lowering=False)

    x_fm = nc.dram_tensor("x_fm", [D, T], F32, kind="ExternalInput").ap()
    wq = nc.dram_tensor("wq", [D, D], F32, kind="ExternalInput").ap()
    wk = nc.dram_tensor("wk", [D, D], F32, kind="ExternalInput").ap()
    wv = nc.dram_tensor("wv", [D, D], F32, kind="ExternalInput").ap()
    wo = nc.dram_tensor("wo", [D, D], F32, kind="ExternalInput").ap()
    w1t = nc.dram_tensor("w1t", [D, FF], F32, kind="ExternalInput").ap()
    w2p = nc.dram_tensor("w2p", [DC, 128, FC * 128], F32, kind="ExternalInput").ap()
    wrt = nc.dram_tensor("wrt", [D, NB], F32, kind="ExternalInput").ap()
    bq2 = nc.dram_tensor("bq2", [128, 16], F32, kind="ExternalInput").ap()
    bk2 = nc.dram_tensor("bk2", [128, 16], F32, kind="ExternalInput").ap()
    bv2 = nc.dram_tensor("bv2", [128, 16], F32, kind="ExternalInput").ap()
    bo2 = nc.dram_tensor("bo2", [128, 16], F32, kind="ExternalInput").ap()
    brr = nc.dram_tensor("brr", [1, NB], F32, kind="ExternalInput").ap()
    causal = nc.dram_tensor("causal", [128, 4 * TILE], F32, kind="ExternalInput").ap()
    identf = nc.dram_tensor("identf", [128, 128], F32, kind="ExternalInput").ap()
    self_ = nc.dram_tensor("self_", [NB, NB * 128], F32, kind="ExternalInput").ap()

    y = nc.dram_tensor("y", [D, T], F32, kind="ExternalOutput").ap()

    s_w1 = nc.dram_tensor("s_w1", [D, FF], BF16, kind="Internal").ap()
    s_w2 = nc.dram_tensor("s_w2", [DC, 128, FC * 128], BF16, kind="Internal").ap()
    s_xn = nc.dram_tensor("s_xn", [D, T], BF16, kind="Internal").ap()
    s_q = nc.dram_tensor("s_q", [D, T], BF16, kind="Internal").ap()
    s_k = nc.dram_tensor("s_k", [D, T], BF16, kind="Internal").ap()
    s_v = nc.dram_tensor("s_v", [NH, 128, T // 128, DH], BF16, kind="Internal").ap()
    s_attn = nc.dram_tensor("s_attn", [D, T], BF16, kind="Internal").ap()
    s_ao = nc.dram_tensor("s_ao", [D, T], F32, kind="Internal").ap()

    p = Prog()

    with ExitStack() as gctx:
        sbuf = lambda n, s, d: gctx.enter_context(nc.sbuf_tensor(n, s, d))
        mksem = lambda n: Sem(gctx.enter_context(nc.semaphore(n)))

        idb = sbuf("idb", [128, 128], BF16)
        selb = sbuf("selb", [NB, NB * 128], BF16)
        wrtb = sbuf("wrtb", [128, DC * NB], BF16)
        brr_bf = sbuf("brr_bf", [1, NB], BF16)
        ones1_bf = sbuf("ones1_bf", [1, 128], BF16)
        ones128_bf = sbuf("ones128_bf", [128, 1], BF16)
        ones1_f = sbuf("ones1_f", [1, 128], F32)
        epsc = sbuf("epsc", [128, 1], F32)
        bqs = sbuf("bqs", [128, 16], F32)
        bks = sbuf("bks", [128, 16], F32)
        bvs = sbuf("bvs", [128, 16], F32)
        bos = sbuf("bos", [128, 16], F32)
        bqsc = sbuf("bqsc", [128, 16], F32)
        bksc = sbuf("bksc", [128, 16], F32)
        mask_fm = sbuf("mask_fm", [NB, T], BF16)
        qknorm = sbuf("qknorm", [1, 2], F32)
        nrm2 = sbuf("nrm2", [1, 2], F32)
        nrm_row = sbuf("nrm_row", [1, TILE], F32)
        negrow = sbuf("negrow", [1, 1], F32)
        negb = sbuf("negb", [128, 1], F32)
        q2c = [sbuf("q2c0", [128, TILE], BF16), sbuf("q2c1", [128, TILE], BF16)]

        g_ld = mksem("g_ld")
        g_dve = mksem("g_dve")
        phase = mksem("phase")

        def dma(queue, out, in_, sem, k=16):
            t = sem.tgt(k)
            p.q[queue].append(lambda e, _o=out, _i=in_, _h=sem.h, _k=k:
                              e.dma_start(out=_o, in_=_i).then_inc(_h, _k))
            return t

        dma("pool", idb[:], identf[:, :], g_ld)
        t_idb = g_ld.n

        dma("sp", bqs[:], bq2[:, :], g_ld)
        dma("sp", bks[:], bk2[:, :], g_ld)
        dma("sp", bvs[:], bv2[:, :], g_ld)
        dma("sp", bos[:], bo2[:, :], g_ld)
        t_pro = g_ld.n

        def dve_op(f, sem=None):
            if sem is None:
                p.dve(f)
                return None
            t = sem.tgt(1)
            p.dve(lambda e, _f=f, _h=sem.h: _f(e).then_inc(_h, 1))
            return t

        p.wait("dve", g_ld, t_pro)
        dve_op(lambda e: e.memset(qknorm[:], 0.0))
        dve_op(lambda e: e.memset(ones1_bf[:], 1.0))
        dve_op(lambda e: e.memset(ones128_bf[:], 1.0))
        dve_op(lambda e: e.memset(ones1_f[:], 1.0))
        dve_op(lambda e: e.memset(epsc[:], EPS))
        dve_op(lambda e: e.tensor_scalar_mul(bqsc[:], bqs[:], ALPHA))
        t_bias = dve_op(lambda e: e.tensor_scalar_mul(bksc[:], bks[:], ALPHA), g_dve)

        s_w1_ld = mksem("s_w1_ld")
        s_w2_ld = mksem("s_w2_ld")

        w_ld = mksem("w_ld")
        xb_ld = [mksem("xb_ld0"), mksem("xb_ld1")]
        xn_sp = mksem("xn_sp")
        qk_st = mksem("qk_st")
        a_pe = mksem("a_pe")
        a_act = mksem("a_act")
        a_dve = mksem("a_dve")
        xn_rld = [mksem("xn_rld0"), mksem("xn_rld1")]
        k_ld = [mksem("k_ld0"), mksem("k_ld1")]
        v_ld = [mksem("v_ld0"), mksem("v_ld1")]

        with ExitStack() as actx:
            asb = lambda n, s, d: actx.enter_context(nc.sbuf_tensor(n, s, d))
            aps = lambda n, s, d: actx.enter_context(nc.psum_tensor(n, s, d))
            wslotA = asb("wslotA", [128, DC * D], BF16)
            wslotB = asb("wslotB", [128, DC * D], BF16)
            xnf = [asb("xnf0", [128, DC * TILE], BF16), asb("xnf1", [128, DC * TILE], BF16)]
            xsq = asb("xsq", [128, 8 * TILE], BF16)
            t1 = asb("t1", [128, 2 * TILE], BF16)
            mrs = asb("mrs", [128, 2 * TILE], BF16)
            rows = asb("rows", [1, 3 * TILE], F32)
            rows_bf = asb("rows_bf", [1, 2 * TILE], BF16)
            r_sb = asb("r_sb", [128, NB], F32)
            max8 = asb("max8", [128, 8], F32)
            mask_tm = asb("mask_tm", [128, NB], BF16)
            qkv_st = [asb("qkv_st0", [128, TILE], BF16), asb("qkv_st1", [128, TILE], BF16)]
            vt_st = [asb("vt_st0", [128, TILE], BF16), asb("vt_st1", [128, TILE], BF16)]
            jnkd = asb("jnkd", [128, 1], F32)
            jnka = asb("jnka", [128, 1], F32)

            ps_qkv = [aps("ps_qkv0", [128, TILE], F32), aps("ps_qkv1", [128, TILE], F32)]
            ps_r = aps("ps_r", [128, NB], F32)
            ps_mT = aps("ps_mT", [128, TILE], BF16)
            ps_bc = aps("ps_bc", [128, TILE], F32)
            ps_sum = aps("ps_sum", [1, TILE], F32)
            ps_sq = aps("ps_sq", [1, TILE], F32)
            ps_nrm = aps("ps_nrm", [1, TILE], F32)

            def dve_mile():
                t = a_dve.tgt(1)
                p.dve(lambda e: e.tensor_copy(out=jnkd[:], in_=epsc[:]).then_inc(a_dve.h, 1))
                return t

            def dve_fence():
                t = dve_mile()
                p.wait("dve", a_dve, t)

            def act_mile():
                t = a_act.tgt(1)
                p.act(lambda e: e.copy(out=jnka[:], in_=epsc[:]).then_inc(a_act.h, 1))
                return t

            dma("pool", selb[:], self_[:, :], g_ld)
            dma("pool", brr_bf[:], brr[:, :], g_ld)
            for dc in range(DC):
                dma("pool", wrtb[:, dc * NB:(dc + 1) * NB],
                    wrt[dc * 128:(dc + 1) * 128, :], g_ld)
            t_pro_pool = g_ld.n

            xb_half = {}
            sq_mile = {}
            sq_pass_pe = {}
            t_statdone = {}
            t_rows_free = {-1: 0}
            t_rstd_dve = {}
            t_norm_done = {0: 0}
            t_mrs_ok = {}
            st_bc_free = [0]
            st_bc_pe = [0]
            st_r_evac = [0]
            st_mask_dve = [0]
            st_maskT_pe = [0]
            st_maskT_evac = [0]

            xnf_free_t = [0, 0]
            xn_spill_t = [0, 0]

            def emit_xb_load(t):
                nx = t % 2
                p.wait("pool", a_pe, xnf_free_t[nx])
                p.wait("pool", xn_sp, xn_spill_t[nx])
                hts = []
                for half in range(2):
                    tt = dma("pool",
                             xnf[nx][:].rearrange("p (c t) -> p c t", c=DC)[:, half * 8:(half + 1) * 8, :],
                             x_fm.rearrange("(c p) t -> p c t", p=128)[:, half * 8:(half + 1) * 8,
                                                                      t * TILE:(t + 1) * TILE],
                             xb_ld[nx])
                    hts.append(tt)
                xb_half[t] = hts

            def emit_squares(t, grp):
                nx = t % 2
                half = 0 if grp < 2 else 1
                p.wait("act", xb_ld[nx], xb_half[t][half])
                prev = (t, grp - 2) if grp >= 2 else (t - 1, grp + 2)
                p.wait("act", a_pe, sq_pass_pe.get(prev, 0))
                for c4 in range(4):
                    c = grp * 4 + c4
                    slot = (grp % 2) * 4 + c4
                    p.act(lambda e, _nx=nx, _c=c, _s=slot: e.activation(
                        out=xsq[:, _s * TILE:(_s + 1) * TILE],
                        in_=xnf[_nx][:, _c * TILE:(_c + 1) * TILE], func=AF.Square))
                sq_mile[(t, grp)] = act_mile()

            def emit_sum_pass(t, c):
                nx = t % 2
                p.wait("te", xb_ld[nx], xb_half[t][0 if c < 8 else 1])
                if c == 0:
                    p.wait("te", a_dve, t_rows_free[t - 1])
                if c == DC - 1:
                    a_pe.tgt(1)
                    p.te(lambda e, _nx=nx, _c=c: e.matmul(
                        ps_sum[0:1, :], ones128_bf[:, 0:1],
                        xnf[_nx][:, _c * TILE:(_c + 1) * TILE],
                        start=(_c == 0), stop=True).then_inc(a_pe.h, 1))
                else:
                    p.te(lambda e, _nx=nx, _c=c: e.matmul(
                        ps_sum[0:1, :], ones128_bf[:, 0:1],
                        xnf[_nx][:, _c * TILE:(_c + 1) * TILE],
                        start=(_c == 0), stop=False))

            def emit_sq_pass(t, c):
                grp = c // 4
                slot = (grp % 2) * 4 + c % 4
                p.wait("te", a_act, sq_mile[(t, grp)])
                if c == 0:
                    p.wait("te", a_dve, t_rows_free[t - 1])
                if c % 4 == 3:
                    tp = a_pe.tgt(1)
                    p.te(lambda e, _s=slot, _c=c: e.matmul(
                        ps_sq[0:1, :], ones128_bf[:, 0:1],
                        xsq[:, _s * TILE:(_s + 1) * TILE],
                        start=(_c == 0), stop=(_c == DC - 1)).then_inc(a_pe.h, 1))
                    sq_pass_pe[(t, grp)] = tp
                    if c == DC - 1:
                        t_statdone[t] = tp
                else:
                    p.te(lambda e, _s=slot, _c=c: e.matmul(
                        ps_sq[0:1, :], ones128_bf[:, 0:1],
                        xsq[:, _s * TILE:(_s + 1) * TILE],
                        start=(_c == 0), stop=False))

            def emit_rows(t):
                p.wait("dve", a_pe, t_statdone[t])
                p.wait("dve", a_pe, st_bc_pe[0])
                dve_op(lambda e: e.tensor_scalar_mul(rows_bf[0:1, 0:TILE], ps_sum[0:1, :], 1.0 / D))
                dve_op(lambda e: e.tensor_scalar_mul(rows[0:1, 0:TILE], ps_sum[0:1, :], 1.0 / D))
                dve_fence()
                dve_op(lambda e: e.tensor_mul(rows[0:1, TILE:2 * TILE],
                                              rows[0:1, 0:TILE], rows[0:1, 0:TILE]))
                dve_fence()
                dve_op(lambda e: e.scalar_tensor_tensor(
                    out=rows[0:1, 0:TILE], in0=ps_sq[0:1, :], scalar=1.0 / D,
                    in1=rows[0:1, TILE:2 * TILE], op0=ALU.mult, op1=ALU.subtract))
                t_rows_free[t] = dve_mile()
                p.wait("act", a_dve, t_rows_free[t])
                p.act(lambda e: e.activation(out=rows[0:1, TILE:2 * TILE],
                                             in_=rows[0:1, 0:TILE],
                                             func=AF.Sqrt, bias=epsc[0:1, 0:1]))
                ta = act_mile()
                p.wait("dve", a_act, ta)
                dve_op(lambda e: e.reciprocal(rows[0:1, 2 * TILE:3 * TILE],
                                              rows[0:1, TILE:2 * TILE]))
                dve_fence()
                dve_op(lambda e: e.tensor_copy(out=rows_bf[0:1, TILE:2 * TILE],
                                               in_=rows[0:1, 2 * TILE:3 * TILE]))
                t_rstd_dve[t] = dve_mile()

            def emit_bcast(t):
                p.wait("te", a_dve, t_rows_free[t])
                p.wait("te", a_act, st_bc_free[0])
                tp = a_pe.tgt(1)
                p.te(lambda e: e.matmul(ps_bc[:, :], ones1_bf[0:1, :], rows_bf[0:1, 0:TILE],
                                        start=True, stop=True).then_inc(a_pe.h, 1))
                p.wait("act", a_pe, tp)
                p.wait("act", a_dve, t_norm_done.get(t - 1, 0))
                p.act(lambda e: e.copy(out=mrs[:, 0:TILE], in_=ps_bc[:, :]))
                tm = act_mile()
                p.wait("te", a_act, tm)
                p.wait("te", a_dve, t_rstd_dve[t])
                tp2 = a_pe.tgt(1)
                p.te(lambda e: e.matmul(ps_bc[:, :], ones1_bf[0:1, :],
                                        rows_bf[0:1, TILE:2 * TILE],
                                        start=True, stop=True).then_inc(a_pe.h, 1))
                st_bc_pe[0] = tp2
                p.wait("act", a_pe, tp2)
                p.act(lambda e: e.copy(out=mrs[:, TILE:2 * TILE], in_=ps_bc[:, :]))
                st_bc_free[0] = act_mile()
                t_mrs_ok[t] = st_bc_free[0]

            def emit_norm_quarter(t, qtr):
                nx = t % 2
                p.wait("dve", a_act, t_mrs_ok[t])
                p.wait("dve", a_pe, t_statdone[t])
                p.wait("dve", a_act, sq_mile[(t, 3)])
                for pr in range(2):
                    for c2 in range(2):
                        c = qtr * 4 + pr * 2 + c2
                        p.dve(lambda e, _nx=nx, _c=c, _c2=c2: e.tensor_tensor(
                            out=t1[:, _c2 * TILE:(_c2 + 1) * TILE],
                            in0=xnf[_nx][:, _c * TILE:(_c + 1) * TILE],
                            in1=mrs[:, 0:TILE], op=ALU.subtract))
                    dve_fence()
                    for c2 in range(2):
                        c = qtr * 4 + pr * 2 + c2
                        p.dve(lambda e, _nx=nx, _c=c, _c2=c2: e.tensor_tensor(
                            out=xnf[_nx][:, _c * TILE:(_c + 1) * TILE],
                            in0=t1[:, _c2 * TILE:(_c2 + 1) * TILE],
                            in1=mrs[:, TILE:2 * TILE], op=ALU.mult))
                if qtr == 3:
                    t_norm_done[t] = dve_mile()

            xn_spill_tile = {}

            def emit_spill(t):
                nx = t % 2
                p.wait("sp", a_dve, t_norm_done[t])
                for half in range(2):
                    xn_spill_t[nx] = dma(
                        "sp",
                        s_xn.rearrange("(c p) t -> p c t", p=128)[:, half * 8:(half + 1) * 8,
                                                                 t * TILE:(t + 1) * TILE],
                        xnf[nx][:].rearrange("p (c t) -> p c t", c=DC)[:, half * 8:(half + 1) * 8, :],
                        xn_sp)
                xn_spill_tile[t] = xn_spill_t[nx]

            def emit_router_sub(t, s):
                nx = t % 2
                sub = t * 4 + s
                p.wait("te", a_dve, t_norm_done[t])
                p.wait("te", g_ld, t_pro_pool)
                p.wait("te", a_act, st_r_evac[0])
                for dc in range(DC):
                    p.te(lambda e, _s=s, _dc=dc, _nx=nx: e.matmul(
                        ps_r[:], xnf[_nx][:, _dc * TILE + _s * 128: _dc * TILE + (_s + 1) * 128],
                        wrtb[:, _dc * NB:(_dc + 1) * NB], start=(_dc == 0), stop=False))
                tp_r = a_pe.tgt(1)
                p.te(lambda e: e.matmul(ps_r[:], ones1_bf[0:1, :], brr_bf[0:1, :],
                                        start=False, stop=True).then_inc(a_pe.h, 1))
                p.wait("act", a_pe, tp_r)
                p.wait("act", a_dve, st_mask_dve[0])
                p.act(lambda e: e.copy(out=r_sb[:], in_=ps_r[:]))
                ta_r = act_mile()
                st_r_evac[0] = ta_r
                p.wait("dve", a_act, ta_r)
                p.wait("dve", a_pe, st_maskT_pe[0])
                dve_op(lambda e: e.max(out=max8[:], in_=r_sb[:]))
                dve_fence()
                p.dve(lambda e: e.tensor_scalar(
                    out=mask_tm[:], in0=r_sb[:], scalar1=max8[:, 3:4], scalar2=None,
                    op0=ALU.is_ge))
                td_m = dve_mile()
                st_mask_dve[0] = td_m
                p.wait("te", a_dve, td_m)
                p.wait("te", a_act, st_maskT_evac[0])
                tp_m = a_pe.tgt(1)
                p.te(lambda e: e.transpose(ps_mT[0:NB, 0:128], mask_tm[:],
                                           idb[:]).then_inc(a_pe.h, 1))
                st_maskT_pe[0] = tp_m
                p.wait("act", a_pe, tp_m)
                p.act(lambda e, _sub=sub: e.copy(
                    out=mask_fm[:, _sub * 128:(_sub + 1) * 128], in_=ps_mT[0:NB, 0:128]))
                st_maskT_evac[0] = act_mile()

            qkv_free_t = [0, 0]
            vt_free_t = [0, 0]
            ps_qkv_free = [0, 0]
            ps_vt_free = [0]
            k0_st = [0]
            v0_st = [0]
            nrm_sq_free = [0, 0]
            nrm_row_evac = 0
            pending_norm = [None, None]

            def flush_norm(par2, wi):
                nonlocal nrm_row_evac
                if pending_norm[par2] is None:
                    return
                td_sq = pending_norm[par2]
                pending_norm[par2] = None
                p.wait("te", a_dve, td_sq)
                p.wait("te", a_act, nrm_row_evac)
                tp_n = a_pe.tgt(1)
                p.te(lambda e, _p2=par2: e.matmul(
                    ps_nrm[0:1, :], ones128_bf[:, 0:1], q2c[_p2][:],
                    start=True, stop=True).then_inc(a_pe.h, 1))
                nrm_sq_free[par2] = tp_n
                p.wait("act", a_pe, tp_n)
                p.act(lambda e: e.copy(out=nrm_row[0:1, :], in_=ps_nrm[0:1, :]))
                nrm_row_evac = act_mile()
                p.wait("dve", a_act, nrm_row_evac)
                dve_op(lambda e: e.reduce_max(nrm2[0:1, 0:1], nrm_row[0:1, :], axis=AX.X))
                dve_fence()
                dve_op(lambda e, _wi=wi: e.tensor_tensor(
                    out=qknorm[0:1, _wi:_wi + 1], in0=qknorm[0:1, _wi:_wi + 1],
                    in1=nrm2[0:1, 0:1], op=ALU.max))
                dve_fence()

            def emit_qkv(j, px, kind, wsl, hook=None):
                dst = {"q": s_q, "k": s_k, "v": None}[kind]
                bias = {"q": bqsc, "k": bksc, "v": bvs}[kind]
                scale = ALPHA if kind in ("q", "k") else 1.0
                wi = 0 if kind == "q" else 1
                for oc in range(DC):
                    par2 = oc % 2
                    if hook is not None:
                        hook(oc)
                    if kind in ("q", "k"):
                        flush_norm(par2, wi)
                    p.wait("te", a_act, ps_qkv_free[par2])
                    for dc in range(DC):
                        last = dc == DC - 1
                        if last:
                            a_pe.tgt(1)
                            p.te(lambda e, _oc=oc, _dc=dc, _px=px, _p2=par2, _w=wsl: e.matmul(
                                ps_qkv[_p2][:],
                                _w[:, _dc * D + _oc * 128: _dc * D + (_oc + 1) * 128],
                                xnf[_px][:, _dc * TILE:(_dc + 1) * TILE],
                                start=(_dc == 0), stop=True).then_inc(a_pe.h, 1))
                        else:
                            p.te(lambda e, _oc=oc, _dc=dc, _px=px, _p2=par2, _w=wsl: e.matmul(
                                ps_qkv[_p2][:],
                                _w[:, _dc * D + _oc * 128: _dc * D + (_oc + 1) * 128],
                                xnf[_px][:, _dc * TILE:(_dc + 1) * TILE],
                                start=(_dc == 0), stop=False))
                    t_mm = a_pe.n
                    p.wait("act", a_pe, t_mm)
                    p.wait("act", qk_st, qkv_free_t[par2])
                    p.act(lambda e, _p2=par2, _oc=oc, _b=bias, _sc=scale: e.activation(
                        out=qkv_st[_p2][:], in_=ps_qkv[_p2][:], func=AF.Identity,
                        bias=_b[:, _oc:_oc + 1], scale=_sc))
                    ta_e = act_mile()
                    ps_qkv_free[par2] = ta_e
                    if kind in ("q", "k"):
                        p.wait("sp", a_act, ta_e)
                        qkv_free_t[par2] = dma(
                            "sp", dst[oc * 128:(oc + 1) * 128, j * TILE:(j + 1) * TILE],
                            qkv_st[par2][:], qk_st)
                        if kind == "k" and oc == 0:
                            k0_st[0] = qkv_free_t[par2]
                        p.wait("dve", a_act, ta_e)
                        p.wait("dve", a_pe, nrm_sq_free[par2])
                        p.dve(lambda e, _p2=par2: e.tensor_tensor(
                            out=q2c[_p2][:], in0=qkv_st[_p2][:], in1=qkv_st[_p2][:],
                            op=ALU.mult))
                        pending_norm[par2] = dve_mile()
                    else:
                        p.wait("te", a_act, ta_e)
                        p.wait("te", a_act, ps_vt_free[0])
                        for s4 in range(4):
                            last = s4 == 3
                            if last:
                                a_pe.tgt(1)
                                p.te(lambda e, _p2=par2, _s4=s4: e.transpose(
                                    ps_mT[:, _s4 * 128:(_s4 + 1) * 128],
                                    qkv_st[_p2][:, _s4 * 128:(_s4 + 1) * 128],
                                    idb[:]).then_inc(a_pe.h, 1))
                            else:
                                p.te(lambda e, _p2=par2, _s4=s4: e.transpose(
                                    ps_mT[:, _s4 * 128:(_s4 + 1) * 128],
                                    qkv_st[_p2][:, _s4 * 128:(_s4 + 1) * 128],
                                    idb[:]))
                        t_vt = a_pe.n
                        p.wait("act", a_pe, t_vt)
                        p.wait("act", qk_st, vt_free_t[par2])
                        p.act(lambda e, _p2=par2: e.copy(
                            out=vt_st[_p2][:], in_=ps_mT[:, :]))
                        ta_v = act_mile()
                        ps_vt_free[0] = ta_v
                        p.wait("sp", a_act, ta_v)
                        vt_free_t[par2] = dma(
                            "sp",
                            s_v[oc][:, j * 4:(j + 1) * 4, :],
                            vt_st[par2][:].rearrange("p (c f) -> p c f", c=4),
                            qk_st)
                        if oc == 0:
                            v0_st[0] = vt_free_t[par2]
                        qkv_free_t[par2] = 0

            p.wait("te", g_ld, t_idb)
            p.wait("te", g_dve, t_bias)
            p.wait("act", g_dve, t_bias)
            emit_xb_load(0)
            emit_squares(0, 0)
            emit_squares(0, 1)
            for c in range(DC):
                emit_sum_pass(0, c)
            for c in range(8):
                emit_sq_pass(0, c)
            emit_squares(0, 2)
            emit_squares(0, 3)
            for c in range(8, DC):
                emit_sq_pass(0, c)
            emit_rows(0)
            emit_bcast(0)
            for qtr in range(4):
                emit_norm_quarter(0, qtr)
            emit_spill(0)
            for s in range(4):
                emit_router_sub(0, s)

            for dc in range(DC):
                dma("pool", wslotA[:, dc * D:(dc + 1) * D], wq[dc * 128:(dc + 1) * 128, :], w_ld)
            t_wq = w_ld.n
            t_wk_holder = [0]

            def qhook_factory(j):
                t = j + 1
                base_map = {5: 0, 6: 2, 8: 4, 10: 6, 12: 8, 13: 10, 14: 12, 15: 14}

                def hook(oc):
                    if j == 0 and oc == 1:
                        for dc in range(DC):
                            dma("pool", wslotB[:, dc * D:(dc + 1) * D],
                                wk[dc * 128:(dc + 1) * 128, :], w_ld)
                        t_wk_holder[0] = w_ld.n
                    if t >= NTILE:
                        return
                    if oc == 0:
                        emit_xb_load(t)
                    elif oc == 3:
                        emit_squares(t, 0)
                    elif oc == 4:
                        emit_squares(t, 1)
                    elif oc == 7:
                        emit_squares(t, 2)
                    elif oc == 11:
                        emit_squares(t, 3)
                    elif oc in base_map:
                        c = base_map[oc]
                        emit_sum_pass(t, c)
                        emit_sum_pass(t, c + 1)
                        emit_sq_pass(t, c)
                        emit_sq_pass(t, c + 1)
                return hook

            def khook_factory(j):
                t = j + 1

                def hook(oc):
                    if t >= NTILE:
                        return
                    if oc == 0:
                        emit_rows(t)
                    elif oc == 1:
                        emit_bcast(t)
                    elif oc in (3, 4, 5, 6):
                        emit_norm_quarter(t, oc - 3)
                    elif oc == 7:
                        emit_spill(t)
                    elif oc in (9, 11, 13, 15):
                        emit_router_sub(t, (oc - 9) // 2)
                return hook

            t_q_done = 0
            for j in range(NTILE):
                px = j % 2
                p.wait("te", a_dve, t_norm_done[j])
                p.wait("te", w_ld, t_wq)
                emit_qkv(j, px, "q", wslotA, qhook_factory(j))
                if j == NTILE - 1:
                    t_q_done = a_pe.n
                p.wait("te", w_ld, t_wk_holder[0])
                emit_qkv(j, px, "k", wslotB, khook_factory(j))
                xnf_free_t[px] = a_pe.n
            flush_norm(0, 0)
            flush_norm(1, 1)

            p.dve(lambda e: e.tensor_scalar(
                out=negrow[0:1, :], in0=qknorm[0:1, 0:1], scalar1=qknorm[0:1, 1:2],
                scalar2=-0.5, op0=ALU.add, op1=ALU.mult))
            td_nb = dve_mile()
            p.wait("te", a_dve, td_nb)
            tp_nb = a_pe.tgt(1)
            p.te(lambda e: e.matmul(ps_r[:, 0:1], ones1_f[0:1, :], negrow[0:1, :],
                                    start=True, stop=True).then_inc(a_pe.h, 1))
            p.wait("act", a_pe, tp_nb)
            p.act(lambda e: e.copy(out=negb[:], in_=ps_r[:, 0:1]))
            t_negb_a = act_mile()
            p.act(lambda e: e.activation(out=negrow[0:1, :], in_=negrow[0:1, :],
                                         func=AF.Exp, scale=0.0))

            p.wait("pool", a_pe, t_q_done)
            for dc in range(DC):
                dma("pool", wslotA[:, dc * D:(dc + 1) * D], wv[dc * 128:(dc + 1) * 128, :], w_ld)
            t_wv = w_ld.n
            p.wait("te", w_ld, t_wv)
            ps_vt_free[0] = st_maskT_evac[0]
            for j in range(NTILE):
                px = j % 2
                p.wait("pool", a_pe, xnf_free_t[px])
                p.wait("pool", xn_sp, xn_spill_tile[j])
                t_r = 0
                for half in range(2):
                    t_r = dma("pool", xnf[px][:].rearrange("p (c t) -> p c t", c=DC)[:, half * 8:(half + 1) * 8, :],
                              s_xn.rearrange("(c p) t -> p c t", p=128)[:, half * 8:(half + 1) * 8,
                                                                       j * TILE:(j + 1) * TILE],
                              xn_rld[px])
                p.wait("te", xn_rld[px], t_r)
                emit_qkv(j, px, "v", wslotA)
                xnf_free_t[px] = a_pe.n

            p.wait("sp", qk_st, qk_st.n)
            p.wait("sp", xn_sp, xn_sp.n)
            p.sp(lambda e: e.nop().then_inc(phase.h, 1))
            p.te(lambda e: e.nop().then_inc(phase.h, 1))
            p.act(lambda e: e.nop().then_inc(phase.h, 1))
            p.dve(lambda e: e.nop().then_inc(phase.h, 1))
            p.pool(lambda e: e.nop().then_inc(phase.h, 1))
            phase.n += 5
            t_phase_a = phase.n

        b_pe = mksem("b_pe")
        b_act = mksem("b_act")
        b_dve = mksem("b_dve")
        wo_ld = mksem("wo_ld")
        q_ld = [mksem("q_ld0"), mksem("q_ld1"), mksem("q_ld2"), mksem("q_ld3")]
        at_st = mksem("at_st")
        at_ld = [mksem("at_ld0"), mksem("at_ld1")]
        ao_st = mksem("ao_st")

        with ExitStack() as bctx:
            bsb = lambda n, s, d: bctx.enter_context(nc.sbuf_tensor(n, s, d))
            bps = lambda n, s, d: bctx.enter_context(nc.psum_tensor(n, s, d))
            ksb = [bsb("ksb0", [128, T], BF16), bsb("ksb1", [128, T], BF16)]
            vsb = [bsb("vsb0", [128, T], BF16), bsb("vsb1", [128, T], BF16)]
            qsb = [bsb("qsb0", [128, TILE], BF16), bsb("qsb1", [128, TILE], BF16),
                   bsb("qsb2", [128, TILE], BF16), bsb("qsb3", [128, TILE], BF16)]
            o_tmp = [bsb("o_tmp0", [128, TILE], F32), bsb("o_tmp1", [128, TILE], F32)]
            rzb_sb = [bsb("rzb0", [128, TILE], F32), bsb("rzb1", [128, TILE], F32)]
            ones_sq = bsb("ones_sq", [128, 128], BF16)
            cbT = bsb("cbT", [128, 4 * TILE], BF16)
            pTs = bsb("pTs", [128, 4 * T], BF16)
            oout = [bsb("oout0", [128, TILE], BF16), bsb("oout1", [128, TILE], BF16)]
            woB = bsb("woB", [128, DC * D], BF16)
            atb = [bsb("atb0", [128, DC * TILE], BF16), bsb("atb1", [128, DC * TILE], BF16)]
            aost = [bsb("aost0", [128, TILE], F32), bsb("aost1", [128, TILE], F32)]
            jdve = bsb("jdve", [128, 1], F32)
            jact = bsb("jact", [128, 1], F32)

            ps_sT = [bps("ps_sT0", [128, 1024], F32), bps("ps_sT1", [128, 1024], F32)]
            ps_o = [bps("ps_o0", [128, TILE], F32), bps("ps_o1", [128, TILE], F32)]
            ps_z = [bps("ps_z0", [128, TILE], F32), bps("ps_z1", [128, TILE], F32)]

            def bdve_mile():
                t = b_dve.tgt(1)
                p.dve(lambda e: e.tensor_copy(out=jdve[:], in_=epsc[:]).then_inc(b_dve.h, 1))
                return t

            def bdve_fence():
                t = bdve_mile()
                p.wait("dve", b_dve, t)

            def bact_mile():
                t = b_act.tgt(1)
                p.act(lambda e: e.copy(out=jact[:], in_=epsc[:]).then_inc(b_act.h, 1))
                return t

            for key in Prog.KEYS:
                p.wait(key, phase, t_phase_a)
            p.dve(lambda e: e.memset(ones_sq[:], 1.0))

            t_cb = dma("pool", cbT[:], causal[:, :], wo_ld)
            p.wait("te", wo_ld, t_cb)
            cast_pieces = []
            for i4 in range(16):
                cast_pieces.append(lambda _i=i4: dma(
                    "pool", s_w1[_i * 128:(_i + 1) * 128, :], w1t[_i * 128:(_i + 1) * 128, :],
                    s_w1_ld))
            for i4 in range(16):
                cast_pieces.append(lambda _i=i4: dma("pool", s_w2[_i], w2p[_i], s_w2_ld))

            kv_free_pe = [0, 0]
            q_free_pe = [0, 0, 0, 0]
            oout_free = [0, 0]
            ps_sT_free = [0, 0]
            ps_o_free = [0, 0]
            z_free = [0, 0]
            o_used = [0, 0]

            p.wait("act", a_act, t_negb_a)

            t_kh = [0] * NH
            t_vh = [0] * NH

            def load_kv(h):
                hp_ = h % 2
                p.wait("pool", b_pe, kv_free_pe[hp_])
                t_kh[h] = dma("pool", ksb[hp_][:], s_k[h * 128:(h + 1) * 128, :], k_ld[hp_])
                t_vh[h] = dma("pool", vsb[hp_][:].rearrange("p (c f) -> p c f", c=T // 128),
                              s_v[h], v_ld[hp_])

            load_kv(0)
            for dc in range(DC):
                dma("pool", woB[:, dc * D:(dc + 1) * D], wo[dc * 128:(dc + 1) * 128, :], wo_ld)
            t_wo = wo_ld.n
            for h in range(NH):
                hp = h % 2
                if h + 1 < NH:
                    load_kv(h + 1)
                t_k, t_v = t_kh[h], t_vh[h]

                for i in range(NTILE):
                    qp = (h * NTILE + i) % 4
                    p.wait("sp", b_pe, q_free_pe[qp])
                    t_q = dma("sp", qsb[qp][:],
                              s_q[h * 128:(h + 1) * 128, i * TILE:(i + 1) * TILE], q_ld[qp])
                    if cast_pieces:
                        cast_pieces.pop(0)()
                    p.wait("te", q_ld[qp], t_q)

                    bp = (h * NTILE + i) % 2
                    nkc = 4 * (i + 1)
                    npair = nkc // 2
                    pend = []

                    def emit_zo(pd):
                        pr, ws_, wt_ = pd
                        p.wait("te", ws_, wt_)
                        if pr == 0:
                            p.wait("te", b_act, ps_o_free[bp])
                            p.wait("te", b_dve, z_free[bp])
                            p.wait("te", v_ld[hp], t_v)
                        for kidx, kc in enumerate((2 * pr, 2 * pr + 1)):
                            st_ = pr == 0 and kidx == 0
                            sp_ = pr == npair - 1 and kidx == 1
                            p.te(lambda e, _bp=bp, _kc=kc, _st=st_, _sp=sp_: e.matmul(
                                ps_z[_bp][:, :], ones_sq[:, :],
                                pTs[:, _kc * 512:(_kc + 1) * 512],
                                start=_st, stop=_sp))
                            if sp_:
                                b_pe.tgt(1)
                                p.te(lambda e, _bp=bp, _hp=hp, _kc=kc, _st=st_, _sp=sp_: e.matmul(
                                    ps_o[_bp][:], vsb[_hp][:, _kc * 128:(_kc + 1) * 128],
                                    pTs[:, _kc * 512:(_kc + 1) * 512],
                                    start=_st, stop=_sp).then_inc(b_pe.h, 1))
                            else:
                                p.te(lambda e, _bp=bp, _hp=hp, _kc=kc, _st=st_, _sp=sp_: e.matmul(
                                    ps_o[_bp][:], vsb[_hp][:, _kc * 128:(_kc + 1) * 128],
                                    pTs[:, _kc * 512:(_kc + 1) * 512],
                                    start=_st, stop=_sp))

                    for pair in range(npair):
                        slot = pair % 2
                        kc0, kc1 = 2 * pair, 2 * pair + 1
                        diag = pair >= 2 * i
                        p.wait("te", b_act, ps_sT_free[slot])
                        p.wait("te", k_ld[hp], t_k)
                        for kidx, kc in enumerate((kc0, kc1)):
                            last = kidx == 1
                            fin = last and not diag
                            if fin:
                                tp_s = b_pe.tgt(1)
                                p.te(lambda e, _hp=hp, _qp=qp, _kc=kc, _sl=slot, _ki=kidx: e.matmul(
                                    ps_sT[_sl][:, _ki * 512:(_ki + 1) * 512],
                                    ksb[_hp][:, _kc * 128:(_kc + 1) * 128],
                                    qsb[_qp][:], start=True, stop=True).then_inc(b_pe.h, 1))
                            else:
                                p.te(lambda e, _hp=hp, _qp=qp, _kc=kc, _sl=slot, _ki=kidx,
                                     _sp=not diag: e.matmul(
                                    ps_sT[_sl][:, _ki * 512:(_ki + 1) * 512],
                                    ksb[_hp][:, _kc * 128:(_kc + 1) * 128],
                                    qsb[_qp][:], start=True, stop=_sp))
                            if diag:
                                s_idx = (pair - 2 * i) * 2 + kidx
                                if last:
                                    tp_s = b_pe.tgt(1)
                                    p.te(lambda e, _sl=slot, _ki=kidx, _s=s_idx: e.matmul(
                                        ps_sT[_sl][:, _ki * 512:(_ki + 1) * 512],
                                        idb[:], cbT[:, _s * 512:(_s + 1) * 512],
                                        start=False, stop=True).then_inc(b_pe.h, 1))
                                else:
                                    p.te(lambda e, _sl=slot, _ki=kidx, _s=s_idx: e.matmul(
                                        ps_sT[_sl][:, _ki * 512:(_ki + 1) * 512],
                                        idb[:], cbT[:, _s * 512:(_s + 1) * 512],
                                        start=False, stop=True))
                        p.wait("act", b_pe, tp_s)
                        ps_sT_free[slot] = b_act.tgt(1)
                        p.act(lambda e, _sl=slot, _pr=pair: e.activation(
                            out=pTs[:, _pr * 1024:(_pr + 1) * 1024],
                            in_=ps_sT[_sl][:], func=AF.Exp,
                            bias=negb[:, 0:1]).then_inc(b_act.h, 1))
                        pend.append((pair, b_act, ps_sT_free[slot]))
                        if len(pend) > 2:
                            emit_zo(pend.pop(0))
                    while pend:
                        emit_zo(pend.pop(0))
                    t_scores_done = b_pe.n

                    op_ = (h * NTILE + i) % 2
                    p.wait("act", b_pe, t_scores_done)
                    p.wait("act", b_dve, o_used[bp])
                    p.act(lambda e, _bp=bp: e.copy(out=o_tmp[_bp][:], in_=ps_o[_bp][:]))
                    ps_o_free[bp] = bact_mile()
                    p.wait("dve", b_pe, t_scores_done)
                    p.dve(lambda e, _bp=bp: e.reciprocal(rzb_sb[_bp][:], ps_z[_bp][:, :]))
                    z_free[bp] = bdve_mile()
                    p.wait("dve", b_dve, z_free[bp])
                    p.wait("dve", b_act, ps_o_free[bp])
                    p.wait("dve", at_st, oout_free[op_])
                    p.dve(lambda e, _bp=bp, _op=op_: e.tensor_tensor(
                        out=oout[_op][:], in0=o_tmp[_bp][:], in1=rzb_sb[_bp][:], op=ALU.mult))
                    o_used[bp] = bdve_mile()
                    p.wait("pool", b_dve, o_used[bp])
                    oout_free[op_] = dma(
                        "pool", s_attn[h * 128:(h + 1) * 128, i * TILE:(i + 1) * TILE],
                        oout[op_][:], at_st)
                    if h == NH - 1 and i == 0:
                        p.wait("pool", at_st, at_st.n)
                        for half in range(2):
                            t_at_pre = dma(
                                "pool",
                                atb[0][:].rearrange("p (c t) -> p c t", c=DC)[:, half * 8:(half + 1) * 8, :],
                                s_attn.rearrange("(c p) t -> p c t", p=128)[:, half * 8:(half + 1) * 8,
                                                                           0:TILE],
                                at_ld[0])
                    q_free_pe[qp] = t_scores_done
                kv_free_pe[hp] = b_pe.n

            t_w1cast = s_w1_ld.n
            t_w2cast = s_w2_ld.n
            t_b1_st = at_st.n

            def load_atb(j, gate):
                ap_ = j % 2
                p.wait("pool", at_st, gate)
                p.wait("pool", b_pe, atb_free[ap_])
                t_at = 0
                for half in range(2):
                    t_at = dma("pool",
                               atb[ap_][:].rearrange("p (c t) -> p c t", c=DC)[:, half * 8:(half + 1) * 8, :],
                               s_attn.rearrange("(c p) t -> p c t", p=128)[:, half * 8:(half + 1) * 8,
                                                                          j * TILE:(j + 1) * TILE],
                               at_ld[ap_])
                return t_at

            atb_free = [0, 0]
            aost_free = [0, 0]
            ps_b2_free = list(ps_sT_free)
            p.wait("sp", at_st, t_b1_st)
            p.wait("te", wo_ld, t_wo)
            for j in range(NTILE):
                ap_ = j % 2
                t_at = t_at_pre if j == 0 else load_atb(j, t_b1_st)
                p.wait("te", at_ld[ap_], t_at)
                for dm in range(DC):
                    par = dm % 2
                    p.wait("te", b_act, ps_b2_free[par])
                    for hfc in range(DC):
                        last = hfc == DC - 1
                        if last:
                            tp_b2 = b_pe.tgt(1)
                            p.te(lambda e, _a=ap_, _dm=dm, _h=hfc, _p=par: e.matmul(
                                ps_sT[_p][:, 0:512],
                                woB[:, _h * D + _dm * 128: _h * D + (_dm + 1) * 128],
                                atb[_a][:, _h * TILE:(_h + 1) * TILE],
                                start=(_h == 0), stop=True).then_inc(b_pe.h, 1))
                        else:
                            p.te(lambda e, _a=ap_, _dm=dm, _h=hfc, _p=par: e.matmul(
                                ps_sT[_p][:, 0:512],
                                woB[:, _h * D + _dm * 128: _h * D + (_dm + 1) * 128],
                                atb[_a][:, _h * TILE:(_h + 1) * TILE],
                                start=(_h == 0), stop=False))
                    p.wait("act", b_pe, tp_b2)
                    p.wait("act", ao_st, aost_free[par])
                    p.act(lambda e, _p=par, _dm=dm: e.activation(
                        out=aost[_p][:], in_=ps_sT[_p][:, 0:512], func=AF.Identity,
                        bias=bos[:, _dm:_dm + 1]))
                    ps_b2_free[par] = bact_mile()
                    p.wait("sp", b_act, ps_b2_free[par])
                    aost_free[par] = dma(
                        "sp", s_ao[dm * 128:(dm + 1) * 128, j * TILE:(j + 1) * TILE],
                        aost[par][:], ao_st)
                atb_free[ap_] = b_pe.n

            p.wait("sp", ao_st, ao_st.n)
            p.sp(lambda e: e.nop().then_inc(phase.h, 1))
            p.te(lambda e: e.nop().then_inc(phase.h, 1))
            p.act(lambda e: e.nop().then_inc(phase.h, 1))
            p.dve(lambda e: e.nop().then_inc(phase.h, 1))
            p.pool(lambda e: e.nop().then_inc(phase.h, 1))
            phase.n += 5
            t_phase_b = phase.n

        c_pe = mksem("c_pe")
        c_act = mksem("c_act")
        c_dve = mksem("c_dve")
        xnb_ld = [mksem("xnb_ld0"), mksem("xnb_ld1")]
        w1_ld = [mksem("w1_ld0"), mksem("w1_ld1")]
        w2_ld = [mksem("w2_ld0"), mksem("w2_ld1")]
        xs_ld = [mksem("xs_ld0"), mksem("xs_ld1")]
        y_st = mksem("y_st")

        with ExitStack() as cctx:
            csb = lambda n, s, d: cctx.enter_context(nc.sbuf_tensor(n, s, d))
            cps = lambda n, s, d: cctx.enter_context(nc.psum_tensor(n, s, d))
            xnc = [csb("xnc0", [128, DC * TILE], BF16), csb("xnc1", [128, DC * TILE], BF16)]
            h_sb = csb("h_sb", [128, FC * TILE], BF16)
            w1g = [csb("w1g0", [128, DC * TILE], BF16), csb("w1g1", [128, DC * TILE], BF16)]
            w2c = [csb("w2c0", [128, FC * 128], BF16), csb("w2c1", [128, FC * 128], BF16)]
            mbb = [csb("mbb0", [128, TILE], BF16), csb("mbb1", [128, TILE], BF16)]
            gel = [csb("gel0", [128, TILE], BF16), csb("gel1", [128, TILE], BF16)]
            xsl = [csb("xsl0", [128, TILE], F32), csb("xsl1", [128, TILE], F32)]
            aosl = [csb("aosl0", [128, TILE], F32), csb("aosl1", [128, TILE], F32)]
            accb = [csb("accb0", [128, TILE], F32), csb("accb1", [128, TILE], F32)]
            yst_b = [csb("yst0", [128, TILE], F32), csb("yst1", [128, TILE], F32)]
            jd2 = csb("jd2", [128, 1], F32)
            ja2 = csb("ja2", [128, 1], F32)

            ps_h = [cps("ps_h0", [128, TILE], F32), cps("ps_h1", [128, TILE], F32)]
            ps_mb = cps("ps_mb", [128, TILE], F32)
            ps_y = [cps("ps_y0", [128, TILE], F32), cps("ps_y1", [128, TILE], F32)]

            def cdve_mile():
                t = c_dve.tgt(1)
                p.dve(lambda e: e.tensor_copy(out=jd2[:], in_=epsc[:]).then_inc(c_dve.h, 1))
                return t

            def cact_mile():
                t = c_act.tgt(1)
                p.act(lambda e: e.copy(out=ja2[:], in_=epsc[:]).then_inc(c_act.h, 1))
                return t

            for key in Prog.KEYS:
                p.wait(key, phase, t_phase_b)
            p.wait("sp", s_w1_ld, t_w1cast)
            p.wait("sp", s_w2_ld, t_w2cast)

            xnc_free = [0, 0]
            w1_free = [0, 0]
            w2_free = [0, 0]
            mbb_free = [0, 0]
            gel_free = [0, 0]
            ps_h_free = [0, 0]
            ps_mb_free = 0
            ps_y_free = [0, 0]
            xs_free = [0, 0]
            yst_free = [0, 0]
            h_free_pe = 0

            for j in range(NTILE):
                jp = j % 2
                p.wait("sp", c_pe, xnc_free[jp])
                t_xn = 0
                for half in range(2):
                    t_xn = dma("sp",
                               xnc[jp][:].rearrange("p (c t) -> p c t", c=DC)[:, half * 8:(half + 1) * 8, :],
                               s_xn.rearrange("(c p) t -> p c t", p=128)[:, half * 8:(half + 1) * 8,
                                                                        j * TILE:(j + 1) * TILE],
                               xnb_ld[jp])
                for g in range(NB):
                    wp = g % 2
                    p.wait("sp", c_pe, w1_free[wp])
                    t_w1 = 0
                    for half in range(2):
                        t_w1 = dma("sp",
                                   w1g[wp][:].rearrange("p (c f) -> p c f", c=DC)[:, half * 8:(half + 1) * 8, :],
                                   s_w1.rearrange("(c p) f -> p c f", p=128)[:, half * 8:(half + 1) * 8,
                                                                            g * TILE:(g + 1) * TILE],
                                   w1_ld[wp])
                    p.wait("te", c_act, ps_mb_free)
                    p.wait("te", c_dve, mbb_free[wp])
                    tp_mb = c_pe.tgt(1)
                    p.te(lambda e, _g=g, _j=j: e.matmul(
                        ps_mb[:], selb[0:16, _g * 128:(_g + 1) * 128],
                        mask_fm[0:16, _j * TILE:(_j + 1) * TILE],
                        start=True, stop=True).then_inc(c_pe.h, 1))
                    p.wait("act", c_pe, tp_mb)
                    p.act(lambda e, _wp=wp: e.copy(out=mbb[_wp][:], in_=ps_mb[:]))
                    ps_mb_free = cact_mile()
                    t_mb = ps_mb_free
                    for fc4 in range(4):
                        fc = 4 * g + fc4
                        hp2 = fc % 2
                        p.wait("te", w1_ld[wp], t_w1)
                        p.wait("te", xnb_ld[jp], t_xn)
                        p.wait("te", c_act, ps_h_free[hp2])
                        for dc in range(DC):
                            last = dc == DC - 1
                            if last:
                                tp_h = c_pe.tgt(1)
                                p.te(lambda e, _wp=wp, _f=fc4, _dc=dc, _jp=jp, _h2=hp2: e.matmul(
                                    ps_h[_h2][:],
                                    w1g[_wp][:, _dc * TILE + _f * 128: _dc * TILE + (_f + 1) * 128],
                                    xnc[_jp][:, _dc * TILE:(_dc + 1) * TILE],
                                    start=(_dc == 0), stop=True).then_inc(c_pe.h, 1))
                            else:
                                p.te(lambda e, _wp=wp, _f=fc4, _dc=dc, _jp=jp, _h2=hp2: e.matmul(
                                    ps_h[_h2][:],
                                    w1g[_wp][:, _dc * TILE + _f * 128: _dc * TILE + (_f + 1) * 128],
                                    xnc[_jp][:, _dc * TILE:(_dc + 1) * TILE],
                                    start=(_dc == 0), stop=False))
                        p.wait("act", c_pe, tp_h)
                        p.wait("act", c_dve, gel_free[hp2])
                        p.act(lambda e, _h2=hp2: e.activation(
                            out=gel[_h2][:], in_=ps_h[_h2][:], func=AF.Gelu))
                        ps_h_free[hp2] = cact_mile()
                        p.wait("dve", c_act, ps_h_free[hp2])
                        p.wait("dve", c_act, t_mb)
                        p.wait("dve", c_pe, h_free_pe)
                        p.dve(lambda e, _h2=hp2, _wp=wp, _fc=fc: e.tensor_tensor(
                            out=h_sb[:, _fc * TILE:(_fc + 1) * TILE],
                            in0=gel[_h2][:], in1=mbb[_wp][:], op=ALU.mult))
                        tmil = cdve_mile()
                        gel_free[hp2] = tmil
                        if fc4 == 3:
                            mbb_free[wp] = tmil
                    w1_free[wp] = c_pe.n
                t_h_done = c_dve.n

                for dm in range(DC):
                    wp2 = dm % 2
                    p.wait("sp", c_pe, w2_free[wp2])
                    t_w2 = dma("sp", w2c[wp2][:], s_w2[dm], w2_ld[wp2])
                    p.wait("pool", c_dve, xs_free[wp2])
                    dma("pool", xsl[wp2][:],
                        x_fm[dm * 128:(dm + 1) * 128, j * TILE:(j + 1) * TILE], xs_ld[wp2])
                    t_xs = dma("pool", aosl[wp2][:],
                               s_ao[dm * 128:(dm + 1) * 128, j * TILE:(j + 1) * TILE], xs_ld[wp2])
                    p.wait("te", w2_ld[wp2], t_w2)
                    p.wait("te", c_dve, t_h_done)
                    p.wait("te", c_dve, ps_y_free[wp2])
                    for fc in range(FC):
                        last = fc == FC - 1
                        if last:
                            tp_y = c_pe.tgt(1)
                            p.te(lambda e, _w=wp2, _fc=fc: e.matmul(
                                ps_y[_w][:], w2c[_w][:, _fc * 128:(_fc + 1) * 128],
                                h_sb[:, _fc * TILE:(_fc + 1) * TILE],
                                start=(_fc == 0), stop=True).then_inc(c_pe.h, 1))
                        else:
                            p.te(lambda e, _w=wp2, _fc=fc: e.matmul(
                                ps_y[_w][:], w2c[_w][:, _fc * 128:(_fc + 1) * 128],
                                h_sb[:, _fc * TILE:(_fc + 1) * TILE],
                                start=(_fc == 0), stop=False))
                    w2_free[wp2] = c_pe.n
                    p.wait("dve", xs_ld[wp2], t_xs)
                    p.dve(lambda e, _w=wp2: e.tensor_tensor(
                        out=accb[_w][:], in0=xsl[_w][:], in1=aosl[_w][:], op=ALU.add))
                    xs_free[wp2] = cdve_mile()
                    p.wait("dve", c_dve, xs_free[wp2])
                    p.wait("dve", c_pe, tp_y)
                    p.wait("dve", y_st, yst_free[wp2])
                    p.dve(lambda e, _w=wp2: e.tensor_tensor(
                        out=yst_b[_w][:], in0=ps_y[_w][:], in1=accb[_w][:], op=ALU.add))
                    ps_y_free[wp2] = cdve_mile()
                    p.wait("pool", c_dve, ps_y_free[wp2])
                    yst_free[wp2] = dma(
                        "pool", y[dm * 128:(dm + 1) * 128, j * TILE:(j + 1) * TILE],
                        yst_b[wp2][:], y_st)
                h_free_pe = c_pe.n
                xnc_free[jp] = c_pe.n

            p.wait("pool", y_st, y_st.n)
            p.wait("sp", y_st, y_st.n)

        with nc.Block() as blk:
            @blk.tensor
            def _(e):
                for f in p.q["te"]:
                    f(e)

            @blk.scalar
            def _(e):
                for f in p.q["act"]:
                    f(e)

            @blk.vector
            def _(e):
                for f in p.q["dve"]:
                    f(e)

            @blk.gpsimd
            def _(e):
                for f in p.q["pool"]:
                    f(e)

            @blk.sync
            def _(e):
                for f in p.q["sp"]:
                    f(e)

    return nc


def _host_prep(inputs):
    x = np.ascontiguousarray(inputs["L0_attn_query"])
    shared = {
        "wq": np.ascontiguousarray(inputs["L0_attn_w_q"]),
        "wk": np.ascontiguousarray(inputs["L0_attn_w_k"]),
        "wv": np.ascontiguousarray(inputs["L0_attn_w_v"]),
        "wo": np.ascontiguousarray(inputs["L0_attn_w_o"]),
        "w1t": np.ascontiguousarray(inputs["L0_ffn_W1"].T),
        "w2p": np.ascontiguousarray(
            inputs["L0_ffn_W2"].reshape(DC, 128, FC, 128).transpose(0, 3, 2, 1)
            .reshape(DC, 128, FC * 128)),
        "wrt": np.ascontiguousarray(inputs["L0_ffn_Wr"].T),
        "bq2": np.ascontiguousarray(inputs["L0_attn_b_q"].reshape(16, 128).T),
        "bk2": np.ascontiguousarray(inputs["L0_attn_b_k"].reshape(16, 128).T),
        "bv2": np.ascontiguousarray(inputs["L0_attn_b_v"].reshape(16, 128).T),
        "bo2": np.ascontiguousarray(inputs["L0_attn_b_o"].reshape(16, 128).T),
        "brr": np.ascontiguousarray(inputs["L0_ffn_br"].reshape(1, NB)),
    }
    cbias = np.zeros((128, 4, TILE), dtype=np.float32)
    pp = np.arange(128)[:, None]
    ff = np.arange(TILE)[None, :]
    for s in range(4):
        cbias[:, s, :] = np.where(ff >= s * 128 + pp, 0.0, -200.0).astype(np.float32)
    shared["causal"] = np.ascontiguousarray(cbias.reshape(128, 4 * TILE))
    shared["identf"] = np.eye(128, dtype=np.float32)
    selm = np.zeros((NB, NB * 128), dtype=np.float32)
    for e_ in range(NB):
        selm[e_, e_ * 128:(e_ + 1) * 128] = 1.0
    shared["self_"] = selm
    per_core = []
    for c in range(8):
        m = dict(shared)
        m["x_fm"] = np.ascontiguousarray(x[c].T)
        per_core.append(m)
    return per_core


def kernel(**inputs):
    per_core = _host_prep(inputs)
    nc = build(debug=False)
    res = run_bass_kernel_spmd(nc, per_core, core_ids=list(range(8)))
    out = np.stack([np.ascontiguousarray(r["y"].T) for r in res.results], axis=0)
    return out
